# version 25
# baseline (speedup 1.0000x reference)
import numpy as np
import ml_dtypes
import concourse.bass as bass
import concourse.mybir as mybir
from concourse.bass_utils import run_bass_kernel_spmd
from concourse.bass_core_rust import IndirectOffsetOnAxis

F32 = mybir.dt.float32
F32R = mybir.dt.float32r
BF16 = mybir.dt.bfloat16
I32 = mybir.dt.int32
U8 = mybir.dt.uint8
AF = mybir.ActivationFunctionType
ALU = mybir.AluOpType
AX = mybir.AxisListType

D = 2048
T = 2048
TK = 2304
HALO = 256
DH = 128
NT = 16
NTK = 18
DFF = 8192
NEG = -1.0e30
EPS = 1e-5
BIG = 1.0e6
CAP = 512
HALFCAP = 256
BISECT = 30


class Prog:

    def __init__(self):
        self.steps = []
        self.cnt = {}

    def sem(self, *names):
        for n in names:
            self.cnt.setdefault(n, 0)

    def v(self, s):
        return self.cnt[s]

    ENG_SEM = {"dv": "DV", "ac": "AC", "pl": "PL"}
    SELF_EDGES = {"dv", "ac", "pl"}

    def add(self, eng, emit, waits=(), inc=None):
        auto = self.ENG_SEM.get(eng)
        selfwait = None
        if auto is not None:
            if inc is None:
                inc = (auto, 1)
            assert inc[0] == auto and inc[1] == 1
            if eng in self.SELF_EDGES:
                selfwait = (auto, self.cnt[auto])
        self.steps.append((eng, tuple(waits), emit, inc, selfwait))
        if inc is not None:
            self.cnt[inc[0]] += inc[1]

    def play(self, name, eng, sems):
        names = name if isinstance(name, (set, tuple)) else {name}
        lastw = {}
        for engine, waits, emit, inc, selfwait in self.steps:
            if engine not in names:
                continue
            allw = list(waits)
            if selfwait is not None:
                allw.append(selfwait)
            for s, v in allw:
                if v > 0 and lastw.get(s, -1) < v:
                    eng.wait_ge(sems[s], v)
                    lastw[s] = v
            ins = emit(eng)
            if inc is not None:
                ins.then_inc(sems[inc[0]], inc[1])


def build_nc():
    nc = bass.Bass("TRN2", target_bir_lowering=False)

    def din(name, shape, dt=F32):
        return nc.dram_tensor(name, shape, dt, kind="ExternalInput").ap()

    x_in = din("x_in", [TK, D])
    wqT = din("wqT", [D, D])
    wkvT = din("wkvT", [D, 1024])
    woT = din("woT", [D, D])
    w1h = din("w1h", [128, 16 * 64 * 128])
    w2 = din("w2", [128, 4 * 64 * 512])
    b1r = din("b1r", [128, 64])
    b2row = din("b2row", [1, D])
    wr_row = din("wr_row", [1, D])
    br_in = din("br", [1, 1])
    qk16 = din("qk16", [16, 1])
    cosb_in = din("cosb", [128, TK], BF16)
    sinb_in = din("sinb", [128, TK], BF16)
    kbias_in = din("kbias", [1, TK], BF16)
    identB_in = din("identB", [128, 128], BF16)
    identF_in = din("identF", [128, 128], F32)
    maskAB_in = din("maskAB", [128, 384], BF16)
    ones_bf_in = din("ones_bf", [1, 128], BF16)
    ones_f_in = din("ones_f", [1, 128], F32)
    onescol_in = din("onescol", [128, 1], F32)
    bigoh_in = din("bigoh", [128, 127], BF16)
    lmask_in = din("lmask", [128, 128], F32)
    myrank_in = din("myrank", [128, 1], F32)
    slotrow_in = din("slotrow", [128, 2], I32)
    selq_in = din("selq", [16, D], BF16)

    out = nc.dram_tensor("out", [T, D], F32, kind="ExternalOutput").ap()

    wo_dram = nc.dram_tensor("wo_dram", [T, D], F32).ap()
    x1_dram = nc.dram_tensor("x1_dram", [T, D], BF16).ap()
    log_db = nc.dram_tensor("log_db", [1, T], F32).ap()
    log_gth = nc.dram_tensor("log_gth", [2, T], F32).ap()
    xg_dram = nc.dram_tensor("xg_dram", [CAP, D], F32).ap()
    xg_red = nc.dram_tensor("xg_red", [CAP, D], F32).ap()
    ymine = nc.dram_tensor("ymine", [HALFCAP, D], F32).ap()
    w1bf_dram = nc.dram_tensor("w1bf_dram", [128, 16 * 64 * 128], BF16).ap()
    w2bf_dram = nc.dram_tensor("w2bf_dram", [128, 4 * 64 * 512], BF16).ap()
    y_full = nc.dram_tensor("y_full", [CAP, D], F32).ap()

    p = Prog()
    p.sem("PE", "DV", "AC", "PL", "C0", "C1", "C2",
          "LC", "XT0", "XT1", "WS0", "WS1", "LT", "WR", "X00", "X01",
          "W10", "W11", "W20", "W21", "B2", "WO0", "WO1", "RL0", "RL1",
          "XS0", "XS1", "LG", "LA", "LZ", "SC0", "SC1", "XG", "YS0", "YS1",
          "XB0", "XB1", "XB2", "XB3", "XB4", "XB5", "YB0", "YB1", "YB2",
          "YB3", "OU0", "OU1", "OU2",
          "PC0", "PC1", "PB0", "PB1")

    import contextlib
    stack = contextlib.ExitStack()

    AR_BYTES = 198 * 1024
    arena = stack.enter_context(nc.sbuf_tensor("arena", [128, AR_BYTES], U8))

    def carve(off_kb, shape, dt):
        nb = mybir.dt.size(dt)
        off = int(off_kb * 1024)
        assert off % nb == 0, (off_kb, dt)
        n = 1
        for s in shape[1:]:
            n *= s
        assert off + n * nb <= AR_BYTES, (off_kb, shape, dt)
        ap = arena[0:shape[0], off:off + n * nb].bitcast(dt)
        if len(shape) == 3:
            ap = ap.rearrange("p (a b) -> p a b", b=shape[2])
        return ap

    xnT_own = [carve(4 * d, [128, T], BF16) for d in range(16)]
    cosb = carve(0, [128, TK], BF16)
    sinb = carve(4.5, [128, TK], BF16)
    rs1 = carve(9, [128, T], BF16)
    rs2 = carve(13, [128, T], BF16)
    probs = [carve(17 + 0.75 * i, [128, 384], BF16) for i in range(8)]
    probsT = [carve(189.5 + 0.75 * i, [128, 384], BF16) for i in range(2)]
    kbias = carve(23, [1, TK], BF16)
    selq = carve(27.5, [16, D], BF16)
    wobf = [carve(31.5 + 16 * i, [128, 16, 512], BF16) for i in range(2)]
    wr_bcast = carve(0, [128, D], F32)
    x0st = [carve(8 + 8 * i, [128, D], F32) for i in range(2)]
    wosc2 = [carve(24 + 8 * i, [128, D], F32) for i in range(2)]
    x1bst = [carve(40 + 4 * i, [128, D], BF16) for i in range(2)]
    tmpa = [carve(48 + 8 * i, [128, D], F32) for i in range(2)]
    xn2scr = [carve(0 + 8 * i, [128, D], F32) for i in range(2)] + [
        carve(128 + 8 * i, [128, D], F32) for i in range(2)]
    ybuf = [carve(16 + 8 * i, [128, D], F32) for i in range(2)] + [
        carve(144 + 8 * i, [128, D], F32) for i in range(2)]
    x1bt = [carve(32 + 4 * i, [128, D], BF16) for i in range(3)] + [
        carve(160, [128, D], BF16), carve(164, [128, D], BF16),
        carve(184, [128, D], BF16)]
    yout = [carve(168 + 8 * i, [128, D], F32) for i in range(2)]
    outst = [carve(44 + 8 * i, [128, D], F32) for i in range(2)]
    xgT = [carve(0 + 0.5 * d, [128, HALFCAP], BF16) for d in range(16)]
    xgtm = [carve(16 + 8 * i, [128, D], F32) for i in range(2)]
    h_bf = [carve(32 + 0.5 * f, [128, HALFCAP], BF16) for f in range(64)]
    qfm = [carve(64 + 4 * hh, [128, T], BF16) for hh in range(16)]
    junk = carve(64, [128, D], F32)
    xnst = [carve(72 + 4 * i, [128, D], BF16) for i in range(2)]
    w1st = [carve(88 + 4 * i, [128, 2048], BF16) for i in range(2)]
    w2st = [carve(96 + 4 * i, [128, 2048], BF16) for i in range(2)]
    y_sb = [carve(74 + 2 * i, [128, 512], F32) for i in range(2)]
    junk2 = carve(80, [128, D], F32)
    wos1 = [carve(128 + 2 * i, [128, 512], F32) for i in range(2)]
    wr_rows = carve(132, [1, D], F32)
    b2rows = carve(140, [1, D], F32)
    b2row_bf = carve(148, [1, D], BF16)
    kfm = [carve(128 + 4.5 * o, [128, TK], BF16) for o in range(4)]
    vtm = [carve(146.5 + 1 * t, [128, 512], BF16) for t in range(18)]
    xnT_halo = [carve(164.5 + 0.5 * d, [128, HALO], BF16) for d in range(16)]
    xt = [carve(172.5 + 8 * i, [128, D], F32) for i in range(2)]
    wst_f = [carve(172.5 + 2 * i, [128, 512], F32) for i in range(2)]
    wst_bf = [carve(176.5 + 1 * i, [128, 512], BF16) for i in range(2)]
    ssk = carve(178.5, [4, TK], F32)
    q2st = [carve(187.5 + 1 * i, [128, 512], BF16) for i in range(2)]
    vstage = [carve(189.5 + 1 * i, [128, 512], BF16) for i in range(2)]
    invk_bf = carve(187.5, [4, TK], BF16)
    ssall = carve(164.5, [16, D], F32)
    pcst = [carve(164.5 + 8 * i, [128, D], F32) for i in range(2)]
    pcbf = [carve(180.5 + 4 * i, [128, D], BF16) for i in range(2)]
    invqT = carve(192, [128, 256], F32)
    zerot = carve(196, [128, 512], F32)

    def small(name, shape, dt):
        return stack.enter_context(nc.sbuf_tensor("s_" + name, shape, dt))

    identB = small("identB", [128, 128], BF16)
    identF = small("identF", [128, 128], F32)
    maskAB = small("maskAB", [128, 384], BF16)
    ones_bf = small("ones_bf", [1, 128], BF16)
    ones_f = small("ones_f", [1, 128], F32)
    onescol = small("onescol", [128, 1], F32)
    bigoh = small("bigoh", [128, 127], BF16)
    lmask = small("lmask", [128, 128], F32)
    myrank = small("myrank", [128, 1], F32)
    slotrow = small("slotrow", [128, 2], I32)
    b1rs = small("b1rs", [128, 64], F32)
    brs = small("brs", [1, 1], F32)
    qk16s = small("qk16s", [16, 1], F32)
    logits = small("logits", [128, 16], F32)
    Lall = small("Lall", [128, 32], F32)
    nm2s = small("nm2s", [128, 16], F32)
    r2s = small("r2s", [128, 16], F32)
    mown = small("mown", [128, 16], F32)
    umown = small("umown", [128, 16], F32)
    mfull = small("mfull", [128, 32], F32)
    scanf = small("scanf", [128, 16], F32)
    zer16 = small("zer16", [128, 16], F32)
    slotF = small("slotF", [128, 16], F32)
    slotI = small("slotI", [128, 16], I32)
    cntg = small("cntg", [128, 32], F32)
    cnt = small("cnt", [128, 1], F32)
    totp = small("totp", [128, 1], F32)
    tot0 = small("tot0", [128, 1], F32)
    offs = small("offs", [128, 1], F32)
    thrb = small("thrb", [128, 1], F32)
    lo = small("lo", [1, 1], F32)
    hi = small("hi", [1, 1], F32)
    mid = small("mid", [1, 1], F32)
    dds = small("dds", [1, 1], F32)
    ees = small("ees", [1, 1], F32)
    flag = small("flag", [1, 1], F32)
    colr = small("colr", [128, 1], F32)
    rowr = small("rowr", [1, 128], F32)
    gmax = small("gmax", [1, 1], F32)
    sigbr = small("sigbr", [128, 2], F32)
    sb12 = small("sb12", [1, 2], F32)
    rowsum = [small(f"rowsum{i}", [128, 1], F32) for i in range(8)]
    recip = [small(f"recip{i}", [128, 1], F32) for i in range(8)]
    shs = small("shs", [128, 16], F32)
    qk_bc = small("qk_bc", [128, 16], F32)
    habsn = small("habsn", [128, 16], F32)
    sgncol = small("sgncol", [16, 1], F32)
    qkrow = small("qkrow", [1, 16], F32)
    tsum = [small(f"tsum{i}", [128, 1], F32) for i in range(2)]
    tssq = [small(f"tssq{i}", [128, 1], F32) for i in range(2)]
    trstd = [small(f"trstd{i}", [128, 1], F32) for i in range(2)]
    tbias = [small(f"tbias{i}", [128, 1], F32) for i in range(2)]
    gmcol = [small(f"gmcol{i}", [128, 1], F32) for i in range(3)]
    tsum3 = [small(f"tsum3{i}", [128, 1], F32) for i in range(4)]
    tssq3 = [small(f"tssq3{i}", [128, 1], F32) for i in range(4)]
    trstd3 = [small(f"trstd3{i}", [128, 1], F32) for i in range(4)]
    tbias3 = [small(f"tbias3{i}", [128, 1], F32) for i in range(4)]
    tbias3b = [small(f"tbias3b{i}", [128, 1], F32) for i in range(2)]
    svec = [small(f"svec{i}", [128, 1], F32) for i in range(2)]

    pb = [stack.enter_context(nc.psum_tensor(f"pb{i}", [128, 512], F32))
          for i in range(8)]

    def pbank_bf(i):
        return pb[i][:, :].bitcast(BF16)

    A = p.add

    for dst, src in [(identB, identB_in), (identF, identF_in),
                     (maskAB, maskAB_in), (ones_bf, ones_bf_in), (ones_f, ones_f_in),
                     (onescol, onescol_in), (bigoh, bigoh_in), (lmask, lmask_in),
                     (myrank, myrank_in), (slotrow, slotrow_in), (b1rs, b1r),
                     (brs, br_in), (qk16s, qk16)]:
        A("sy", (lambda d=dst, s=src: lambda e: e.dma_start(out=d[:, :], in_=s))(),
          inc=("LC", 16))
    LC_ALL = p.v("LC")
    A("dv", lambda e: nc.vector.memset(zer16[:, :], 0.0), inc=("DV", 1))
    A("dv", lambda e: nc.vector.memset(zerot[:, :], 0.0), inc=("DV", 1))
    DV_ZERO = p.v("DV")

    ac_apply = {}
    pe_tp = {}
    dv_evac = {}
    dv_evac_bank = {0: DV_ZERO, 1: DV_ZERO}
    for i in range(NTK):
        bi = i % 2
        w = [("AC", ac_apply.get(i - 2, 0))]
        xts = "XT%d" % bi
        A("sy", (lambda i=i, bi=bi: lambda e: e.dma_start(
            out=xt[bi][:, :], in_=x_in[i * 128:(i + 1) * 128, :]))(),
          waits=w, inc=(xts, 16))
        ldv = p.v(xts)
        A("dv", (lambda bi=bi: lambda e: nc.vector.reduce_sum(
            tsum[bi][:, :], xt[bi][:, :], AX.X, negate=True))(),
          waits=[(xts, ldv), ("LC", LC_ALL)], inc=("DV", 1))
        A("dv", (lambda bi=bi: lambda e: nc.vector.tensor_scalar(
            tsum[bi][:, :], tsum[bi][:, :], 1.0 / D, None, ALU.mult))(), inc=("DV", 1))
        dv_nm = p.v("DV")
        A("ac", (lambda bi=bi: lambda e: nc.scalar.activation(
            junk[:, :], xt[bi][:, :], AF.Square, bias=tsum[bi][:, :],
            accum_out=tssq[bi][:, :]))(),
          waits=[("DV", dv_nm), ("LC", LC_ALL)], inc=("AC", 1))
        ac_sq = p.v("AC")
        A("dv", (lambda bi=bi: lambda e: nc.vector.tensor_scalar(
            trstd[bi][:, :], tssq[bi][:, :], 1.0 / D, EPS, ALU.mult, ALU.add))(),
          waits=[("AC", ac_sq)], inc=("DV", 1))
        dv_v = p.v("DV")
        A("ac", (lambda bi=bi: lambda e: nc.scalar.activation(
            trstd[bi][:, :], trstd[bi][:, :], AF.Ln))(),
          waits=[("DV", dv_v)], inc=("AC", 1))
        A("ac", (lambda bi=bi: lambda e: nc.scalar.activation(
            trstd[bi][:, :], trstd[bi][:, :], AF.Exp, scale=-0.5))(), inc=("AC", 1))
        ac_r = p.v("AC")
        A("dv", (lambda bi=bi: lambda e: nc.vector.tensor_mul(
            tbias[bi][:, :], tsum[bi][:, :], trstd[bi][:, :]))(),
          waits=[("AC", ac_r)], inc=("DV", 1))
        dv_b = p.v("DV")
        wv = [("DV", dv_b)]
        if i >= 2:
            wv.append(("PE", pe_tp.get(i - 2, 0)))
        A("ac", (lambda bi=bi: lambda e: nc.scalar.activation(
            xnst[bi][:, :], xt[bi][:, :], AF.Identity,
            bias=tbias[bi][:, :], scale=trstd[bi][:, :]))(),
          waits=wv, inc=("AC", 1))
        ac_apply[i] = p.v("AC")
        for g in range(4):
            bk = g % 2
            for j in range(4):
                d = g * 4 + j
                A("pe", (lambda bi=bi, bk=bk, j=j, d=d: lambda e: nc.tensor.transpose(
                    pbank_bf(bk)[:, j * 128:(j + 1) * 128],
                    xnst[bi][:, d * 128:(d + 1) * 128], identB[:, :]))(),
                  waits=[("AC", ac_apply[i]), ("DV", dv_evac_bank[bk]),
                         ("LC", LC_ALL)],
                  inc=("PE", 1) if j == 3 else None)
            pe_tp[i] = p.v("PE")
            d0 = g * 4
            if i < 2:
                dsts = [xnT_halo[d0 + j][:, i * 128:(i + 1) * 128] for j in range(4)]
            else:
                dsts = [xnT_own[d0 + j][:, (i - 2) * 128:(i - 1) * 128] for j in range(4)]
            for j in range(4):
                A("dv", (lambda bk=bk, j=j, dd=dsts[j]: lambda e: nc.vector.tensor_copy(
                    dd, pbank_bf(bk)[:, j * 128:(j + 1) * 128]))(),
                  waits=[("PE", pe_tp[i])], inc=("DV", 1) if j == 3 else None)
            dv_evac_bank[bk] = p.v("DV")
        dv_evac[i] = p.v("DV")
    DV_P1 = p.v("DV")
    evac_of_tile = dv_evac

    def kv_macro(m):
        if m == 0:
            return [xnT_halo[d][:, 0:HALO] for d in range(16)], HALO, evac_of_tile[1]
        mm = m - 1
        sl = [xnT_own[d][:, mm * 512:(mm + 1) * 512] for d in range(16)]
        return sl, 512, evac_of_tile[2 + mm * 4 + 3]

    def q_macro(m):
        sl = [xnT_own[d][:, m * 512:(m + 1) * 512] for d in range(16)]
        return sl, 512, evac_of_tile[2 + m * 4 + 3]

    pe_after_stage = {}
    kv_cols = [(0, HALO), (HALO, 512), (HALO + 512, 512),
               (HALO + 1024, 512), (HALO + 1536, 512)]
    dv_bankfree = DV_P1
    pe_vtp = {}
    dv_vtm = {}
    sidx = 0
    for o in range(8):
        for q in range(4):
            sbi = sidx % 2
            ws = "WS%d" % sbi
            wv = [("DV", pe_after_stage.get(sidx - 2, (0, 0))[1]),
                  ("PE", pe_after_stage.get(sidx - 2, (0, 0))[0])]
            if sidx < 2:
                wv += [("AC", p.v("AC")), ("DV", DV_P1)]
            A("sy", (lambda o=o, q=q, sbi=sbi: lambda e: e.dma_start(
                out=wst_f[sbi][:, :].rearrange("p (c m) -> p c m", m=128),
                in_=wkvT[q * 512:(q + 1) * 512, o * 128:(o + 1) * 128].rearrange(
                    "(c p) m -> p c m", p=128)))(),
              waits=wv, inc=(ws, 16))
            lwv = p.v(ws)
            A("dv", (lambda sbi=sbi: lambda e: nc.vector.tensor_copy(
                wst_bf[sbi][:, :], wst_f[sbi][:, :]))(),
              waits=[(ws, lwv), ("LC", LC_ALL)], inc=("DV", 1))
            plv = p.v("DV")
            for c in range(4):
                d = q * 4 + c
                for m in range(5):
                    slices, width, rdy = kv_macro(m)
                    first = (d == 0)
                    last = (d == 15)
                    wv = [("DV", plv), ("DV", rdy)]
                    if first:
                        wv.append(("DV", dv_bankfree))
                        wv.append(("AC", p.v("AC")))
                    A("pe", (lambda m=m, c=c, sbi=sbi, sl=slices[d], width=width,
                             first=first, last=last: lambda e: nc.tensor.matmul(
                                 pb[m][:, 0:width],
                                 wst_bf[sbi][:, c * 128:(c + 1) * 128],
                                 sl, start=first, stop=last))(),
                      waits=wv, inc=("PE", 1) if (c == 3 and m == 4) else None)
            pe_after_stage[sidx] = (p.v("PE"), plv)
            sidx += 1
        pe_done_o = p.v("PE")
        if o < 4:
            for m in range(5):
                col0, width = kv_cols[m]
                A("dv", (lambda o=o, m=m, col0=col0, width=width:
                         lambda e: nc.vector.tensor_copy(
                             kfm[o][:, col0:col0 + width], pb[m][:, 0:width]))(),
                  waits=[("PE", pe_done_o)], inc=("DV", 1))
            dv_bankfree = p.v("DV")
        else:
            ov = o - 4
            for m in range(5):
                col0, width = kv_cols[m]
                sbv = m % 2
                bk = 5 + (m % 2)
                A("dv", (lambda m=m, sbv=sbv, width=width: lambda e: nc.vector.tensor_copy(
                    vstage[sbv][:, 0:width], pb[m][:, 0:width]))(),
                  waits=[("PE", pe_done_o), ("PE", pe_vtp.get(m - 2, 0))],
                  inc=("DV", 1))
                dvv = p.v("DV")
                nblk = width // 128
                t0 = col0 // 128
                for c in range(nblk):
                    A("pe", (lambda sbv=sbv, c=c, bk=bk: lambda e: nc.tensor.transpose(
                        pbank_bf(bk)[:, c * 128:(c + 1) * 128],
                        vstage[sbv][:, c * 128:(c + 1) * 128], identB[:, :]))(),
                      waits=[("DV", dvv),
                             ("DV", dv_vtm.get(m - 2, DV_P1 if ov == 0 else 0))],
                      inc=("PE", 1) if c == nblk - 1 else None)
                pe_vtp[m] = p.v("PE")
                for c in range(nblk):
                    A("dv", (lambda c=c, t0=t0, ov=ov, bk=bk: lambda e: nc.vector.tensor_copy(
                        vtm[t0 + c][:, ov * 128:(ov + 1) * 128],
                        pbank_bf(bk)[:, c * 128:(c + 1) * 128]))(),
                      waits=[("PE", pe_vtp[m])],
                      inc=("DV", 1) if c == nblk - 1 else None)
                dv_vtm[m] = p.v("DV")
            dv_bankfree = p.v("DV")
    PE_P2 = p.v("PE")
    DV_P2 = p.v("DV")
    PL_P2 = p.v("PL")

    qss_bank_of_m = [4, 5, 6, 7]
    pe_ssq = {}
    dv_qbankfree = DV_P2
    for hh in range(16):
        for q in range(4):
            sbi = sidx % 2
            ws = "WS%d" % sbi
            A("sy", (lambda hh=hh, q=q, sbi=sbi: lambda e: e.dma_start(
                out=wst_f[sbi][:, :].rearrange("p (c m) -> p c m", m=128),
                in_=wqT[q * 512:(q + 1) * 512, hh * 128:(hh + 1) * 128].rearrange(
                    "(c p) m -> p c m", p=128)))(),
              waits=[("DV", pe_after_stage.get(sidx - 2, (0, 0))[1]),
                     ("PE", pe_after_stage.get(sidx - 2, (0, 0))[0])],
              inc=(ws, 16))
            lwv = p.v(ws)
            A("dv", (lambda sbi=sbi: lambda e: nc.vector.tensor_copy(
                wst_bf[sbi][:, :], wst_f[sbi][:, :]))(),
              waits=[(ws, lwv)], inc=("DV", 1))
            plv = p.v("DV")
            for c in range(4):
                d = q * 4 + c
                for m in range(4):
                    slices, width, rdy = q_macro(m)
                    first = (d == 0)
                    last = (d == 15)
                    wv = [("DV", plv), ("DV", rdy)]
                    if first:
                        wv.append(("DV", dv_qbankfree))
                        wv.append(("AC", p.v("AC")))
                    A("pe", (lambda m=m, c=c, sbi=sbi, sl=slices[d], first=first,
                             last=last: lambda e: nc.tensor.matmul(
                                 pb[m][:, :], wst_bf[sbi][:, c * 128:(c + 1) * 128],
                                 sl, start=first, stop=last))(),
                      waits=wv, inc=("PE", 1) if (c == 3 and m == 3) else None)
            pe_after_stage[sidx] = (p.v("PE"), plv)
            sidx += 1
        pe_done_h = p.v("PE")
        for m in range(4):
            wv = [("PE", pe_done_h), ("PE", pe_ssq.get(hh * 4 + m - 2, 0))]
            if hh == 0:
                wv.append(("DV", DV_P2))
            A("ac", (lambda m=m: lambda e: nc.scalar.activation(
                q2st[m % 2][:, :], pb[m][:, :], AF.Square))(),
              waits=wv, inc=("AC", 1))
            acsq = p.v("AC")
            wv = [("AC", acsq)]
            if hh == 0:
                wv.append(("DV", DV_P2))
            A("pe", (lambda m=m, hh=hh: lambda e: nc.tensor.matmul(
                pb[qss_bank_of_m[m]][0:32, :], bigoh[:, 63 - hh:95 - hh],
                q2st[m % 2][:, :], start=(hh == 0), stop=(hh == 15)))(),
              waits=wv, inc=("PE", 1))
            pe_ssq[hh * 4 + m] = p.v("PE")
            A("dv", (lambda hh=hh, m=m: lambda e: nc.vector.tensor_copy(
                qfm[hh][:, m * 512:(m + 1) * 512], pb[m][:, :]))(),
              waits=[("AC", acsq)], inc=("DV", 1))
        dv_qbankfree = p.v("DV")
    PE_P3 = p.v("PE")
    PL_P3 = p.v("PL")
    for m in range(4):
        A("dv", (lambda m=m: lambda e: nc.vector.tensor_copy(
            ssall[0:16, m * 512:(m + 1) * 512], pb[qss_bank_of_m[m]][0:16, :]))(),
          waits=[("PE", PE_P3)], inc=("DV", 1))
    DV_SSQ = p.v("DV")

    for dst, src in [(cosb, cosb_in), (sinb, sinb_in), (kbias, kbias_in),
                     (selq, selq_in)]:
        A("sy", (lambda d=dst, s=src: lambda e: e.dma_start(out=d[:, :], in_=s))(),
          waits=[("PE", PE_P3)], inc=("LT", 16))
    LW_TRIG = p.v("LT")

    dv_kss = DV_SSQ
    pe_kssm = {}
    for m in range(5):
        col0, width = kv_cols[m]
        for o in range(4):
            j = m * 4 + o
            A("ac", (lambda o=o, col0=col0, width=width: lambda e: nc.scalar.activation(
                q2st[o % 2][:, 0:width], kfm[o][:, col0:col0 + width], AF.Square))(),
              waits=[("DV", DV_P2), ("PE", PE_P3), ("PE", pe_kssm.get(j - 2, 0))],
              inc=("AC", 1))
            acsq = p.v("AC")
            A("pe", (lambda o=o, width=width, m=m: lambda e: nc.tensor.matmul(
                pb[7][0:32, 0:width], bigoh[:, 63 - o:95 - o],
                q2st[o % 2][:, 0:width], start=(o == 0), stop=(o == 3)))(),
              waits=[("AC", acsq), ("DV", dv_kss if o == 0 else 0),
                     ("DV", DV_SSQ if (m == 0 and o == 0) else 0)],
              inc=("PE", 1))
            pe_kssm[j] = p.v("PE")
        pe_kss = p.v("PE")
        A("dv", (lambda m=m, col0=col0, width=width: lambda e: nc.vector.tensor_copy(
            ssk[0:4, col0:col0 + width], pb[7][0:4, 0:width]))(),
          waits=[("PE", pe_kss)], inc=("DV", 1))
        dv_kss = p.v("DV")
    A("dv", lambda e: nc.vector.tensor_scalar_max(ssk[:, :], ssk[:, :], 1e-24),
      waits=[("DV", dv_kss)], inc=("DV", 1))
    dvk1 = p.v("DV")
    A("ac", lambda e: nc.scalar.activation(ssk[:, :], ssk[:, :], AF.Ln),
      waits=[("DV", dvk1)], inc=("AC", 1))
    A("ac", lambda e: nc.scalar.activation(ssk[:, :], ssk[:, :], AF.Exp, scale=-0.5),
      inc=("AC", 1))
    ack = p.v("AC")
    A("dv", lambda e: nc.vector.tensor_copy(invk_bf[:, :], ssk[:, :]),
      waits=[("AC", ack)], inc=("DV", 1))
    DV_INVK = p.v("DV")
    A("dv", lambda e: nc.vector.tensor_scalar_max(ssall[:, :], ssall[:, :], 1e-24),
      waits=[("DV", DV_SSQ)], inc=("DV", 1))
    dvq1 = p.v("DV")
    A("ac", lambda e: nc.scalar.activation(ssall[:, :], ssall[:, :], AF.Ln),
      waits=[("DV", dvq1)], inc=("AC", 1))
    A("ac", lambda e: nc.scalar.activation(ssall[:, :], ssall[:, :], AF.Exp, scale=-0.5),
      inc=("AC", 1))
    acq = p.v("AC")
    A("dv", lambda e: nc.vector.tensor_scalar(
        ssall[:, :], ssall[:, :], qk16s[:, :], None, ALU.mult),
      waits=[("AC", acq)], inc=("DV", 1))
    dvq2 = p.v("DV")
    A("ac", lambda e: nc.scalar.activation(ssall[:, :], ssall[:, :], AF.Abs),
      waits=[("DV", dvq2)], inc=("AC", 1))
    ACQ_ABS = p.v("AC")
    dv_iqt = {6: dv_kss, 7: dv_kss}
    for t in range(16):
        bk = 6 + (t % 2)
        A("pe", (lambda t=t, bk=bk: lambda e: nc.tensor.transpose(
            pb[bk][:, 0:16], ssall[0:16, t * 128:(t + 1) * 128],
            identF[0:16, 0:16]))(),
          waits=[("AC", ACQ_ABS), ("DV", dv_iqt[bk]), ("PE", pe_kss)],
          inc=("PE", 1))
        pev = p.v("PE")
        A("dv", (lambda t=t, bk=bk: lambda e: nc.vector.tensor_copy(
            invqT[:, t * 16:(t + 1) * 16], pb[bk][:, 0:16]))(),
          waits=[("PE", pev)], inc=("DV", 1))
        dv_iqt[bk] = p.v("DV")
    A("pe", lambda e: nc.tensor.transpose(pb[6][0:1, 0:16], qk16s[:, :],
                                          identF[0:16, 0:16]),
      waits=[("LC", LC_ALL), ("DV", dv_iqt[6])], inc=("PE", 1))
    pev = p.v("PE")
    A("dv", lambda e: nc.vector.tensor_copy(qkrow[:, :], pb[6][0:1, 0:16]),
      waits=[("PE", pev)], inc=("DV", 1))
    dvr = p.v("DV")
    A("pe", lambda e: nc.tensor.matmul(pb[6][:, 0:16], ones_f[:, :], qkrow[:, :],
                                       start=True, stop=True),
      waits=[("DV", dvr)], inc=("PE", 1))
    pev = p.v("PE")
    A("dv", lambda e: nc.vector.tensor_copy(qk_bc[:, :], pb[6][:, 0:16]),
      waits=[("PE", pev)], inc=("DV", 1))
    dvq = p.v("DV")
    A("ac", lambda e: nc.scalar.activation(shs[:, :], qk_bc[:, :], AF.Sign),
      waits=[("DV", dvq)], inc=("AC", 1))
    A("ac", lambda e: nc.scalar.activation(habsn[:, :], qk_bc[:, :], AF.Abs),
      inc=("AC", 1))
    achb = p.v("AC")
    A("dv", lambda e: nc.vector.tensor_scalar(habsn[:, :], habsn[:, :], -1.0, None,
                                              ALU.mult),
      waits=[("AC", achb)], inc=("DV", 1))
    A("dv", lambda e: nc.vector.tensor_copy(shs[:, :], shs[:, :]),
      inc=("DV", 1))
    DV_PREP = p.v("DV")

    def rope_k(fm, o):
        for idx, (c0, width) in enumerate(kv_cols):
            bk = 6 + (idx % 2)
            A("pe", (lambda bk=bk, o=o, c0=c0, width=width:
                     lambda e: nc.tensor.matmul(
                         pb[bk][:, 0:width], selq[0:4, o * 128:(o + 1) * 128],
                         invk_bf[0:4, c0:c0 + width], start=True, stop=True))(),
              waits=[("DV", DV_INVK), ("LT", LW_TRIG), ("DV", p.v("DV")),
                     ("DV", DV_PREP)],
              inc=("PE", 1))
            pev = p.v("PE")
            A("dv", (lambda fm=fm, c0=c0, width=width, bk=bk:
                     lambda e: nc.vector.tensor_mul(
                         fm[:, c0:c0 + width], fm[:, c0:c0 + width],
                         pb[bk][:, 0:width]))(),
              waits=[("PE", pev)], inc=("DV", 1))
            A("dv", (lambda fm=fm, c0=c0, width=width:
                     lambda e: nc.vector.tensor_mul(
                         rs1[:, 0:width], fm[:, c0:c0 + width],
                         cosb[:, c0:c0 + width]))(), inc=("DV", 1))
            A("dv", (lambda fm=fm, c0=c0, width=width:
                     lambda e: nc.vector.tensor_mul(
                         rs2[0:64, 0:width], fm[64:128, c0:c0 + width],
                         sinb[64:128, c0:c0 + width]))(), inc=("DV", 1))
            A("dv", (lambda fm=fm, c0=c0, width=width:
                     lambda e: nc.vector.tensor_mul(
                         rs2[64:128, 0:width], fm[0:64, c0:c0 + width],
                         sinb[0:64, c0:c0 + width]))(), inc=("DV", 1))
            A("dv", (lambda fm=fm, c0=c0, width=width: lambda e: nc.vector.tensor_add(
                fm[:, c0:c0 + width], rs1[:, 0:width], rs2[:, 0:width]))(),
              inc=("DV", 1))
        return p.v("DV")

    dv_krope = {}
    for o in range(4):
        dv_krope[o] = rope_k(kfm[o], o)

    qcols = [(0, 2048)]
    ac_exp = {}
    pe_ptp = {}
    dv_ptevac = {}
    dv_avevac = {}
    width = 2048

    def rope_q_t1(hh):
        A("dv", (lambda hh=hh: lambda e: nc.vector.tensor_mul(
            rs1[:, 0:width], qfm[hh][:, 0:width],
            cosb[:, HALO:HALO + width]))(),
          waits=[("DV", DV_PREP), ("LT", LW_TRIG)], inc=("DV", 1))
        return p.v("DV")

    def rope_q_pla(hh, dvt1):
        A("pl", (lambda hh=hh: lambda e: nc.gpsimd.tensor_mul(
            rs2[0:64, 0:width], qfm[hh][64:128, 0:width],
            sinb[64:128, HALO:HALO + width]))(),
          waits=[("DV", DV_PREP), ("LT", LW_TRIG), ("DV", dvt1)],
          inc=("PL", 1))

    def rope_q_plb(hh):
        A("pl", (lambda hh=hh: lambda e: nc.gpsimd.tensor_mul(
            rs2[64:128, 0:width], qfm[hh][0:64, 0:width],
            sinb[0:64, HALO:HALO + width]))(),
          inc=("PL", 1))
        return p.v("PL")

    def rope_q_add(hh, plr):
        A("dv", (lambda hh=hh: lambda e: nc.vector.tensor_add(
            rs1[:, 0:width], rs1[:, 0:width], rs2[:, 0:width]))(),
          waits=[("PL", plr)], inc=("DV", 1))

    def rope_q_fin(hh):
        A("dv", (lambda hh=hh: lambda e: nc.vector.tensor_scalar(
            qfm[hh][:, 0:width], rs1[:, 0:width], shs[:, hh:hh + 1],
            None, ALU.mult))(), inc=("DV", 1))
        return p.v("DV")

    st5 = {}

    def p5_front(it, hh, t, kv, dv_qr):
        sb3 = it % 3
        sb6 = it % 8
        cb = 3 + it % 4
        wv = [("DV", dv_qr[0]), ("PL", dv_qr[1]), ("DV", dv_krope[kv]),
              ("AC", ac_exp.get(it - 3, 0)), ("LT", LW_TRIG)]
        if it < 3:
            wv.append(("DV", DV_SSQ))
        A("pe", (lambda hh=hh, t=t, sb3=sb3, kv=kv: lambda e: nc.tensor.matmul(
            pb[sb3][:, 0:384], qfm[hh][:, t * 128:(t + 1) * 128],
            kfm[kv][:, t * 128:t * 128 + 384], start=True, stop=False))(),
          waits=wv)
        lastm = (t >= 2)
        A("pe", (lambda sb3=sb3, lastm=lastm: lambda e: nc.tensor.matmul(
            pb[sb3][:, 0:384], identB[:, :], maskAB[:, :],
            start=False, stop=lastm))(),
          inc=("PE", 1) if lastm else None)
        if not lastm:
            A("pe", (lambda sb3=sb3, t=t: lambda e: nc.tensor.matmul(
                pb[sb3][:, 0:384], ones_bf[:, :],
                kbias[0:1, t * 128:t * 128 + 384], start=False, stop=True))(),
              inc=("PE", 1))
        pe_sc = p.v("PE")
        A("ac", (lambda sb3=sb3, sb6=sb6, hh=hh, t=t: lambda e: nc.scalar.activation(
            probs[sb6][:, :], pb[sb3][:, 0:384], AF.Exp,
            bias=habsn[:, hh:hh + 1],
            scale=invqT[:, t * 16 + hh:t * 16 + hh + 1],
            accum_out=rowsum[sb6][:, :]))(),
          waits=[("PE", pe_sc), ("PE", pe_ptp.get(it - 8, 0)), ("DV", DV_PREP)],
          inc=("AC", 1))
        ac_exp[it] = p.v("AC")
        A("dv", (lambda sb6=sb6: lambda e: nc.vector.reciprocal(
            recip[sb6][:, :], rowsum[sb6][:, :]))(),
          waits=[("AC", ac_exp[it])], inc=("DV", 1))
        A("dv", (lambda sb6=sb6: lambda e: nc.vector.tensor_scalar(
            probs[sb6][:, :], probs[sb6][:, :], recip[sb6][:, :], None,
            ALU.mult))(),
          inc=("DV", 1))
        st5[it] = (hh, t, kv, sb3, sb6, cb, p.v("DV"))

    st5b = {}

    def p5_back1(it):
        hh, t, kv, sb3, sb6, cb, dv_psc = st5.pop(it)
        sb2 = it % 2
        for c in range(3):
            wv = [("DV", dv_psc), ("DV", dv_ptevac.get(it - 4, 0))]
            if it < 4:
                wv.append(("DV", dv_qbankfree))
            A("pe", (lambda cb=cb, sb6=sb6, c=c: lambda e: nc.tensor.transpose(
                pbank_bf(cb)[:, c * 128:(c + 1) * 128],
                probs[sb6][:, c * 128:(c + 1) * 128], identB[:, :]))(),
              waits=wv, inc=("PE", 1) if c == 2 else None)
        pe_ptp[it] = p.v("PE")
        wv = [("PE", pe_ptp[it])]
        if it < 2:
            wv.append(("DV", dv_krope[3]))
        A("dv", (lambda cb=cb, sb2=sb2: lambda e: nc.vector.tensor_copy(
            probsT[sb2][:, :], pbank_bf(cb)[:, 0:384]))(),
          waits=wv, inc=("DV", 1))
        dv_ptevac[it] = p.v("DV")
        st5b[it] = (hh, t, kv, cb, sb2)

    def p5_back2(it):
        hh, t, kv, cb, sb2 = st5b.pop(it)
        for c in range(3):
            wv = [("DV", dv_ptevac[it]), ("DV", DV_P2)]
            if (it - 4) in dv_avevac:
                wv.append(dv_avevac[it - 4])
            A("pe", (lambda cb=cb, sb2=sb2, c=c, t=t, kv=kv:
                     lambda e: nc.tensor.matmul(
                         pb[cb][:, 256:384],
                         vtm[t + c][:, kv * 128:(kv + 1) * 128],
                         probsT[sb2][:, c * 128:(c + 1) * 128],
                         start=(c == 0), stop=(c == 2)))(),
                  waits=wv, inc=("PE", 1) if c == 2 else None)
        pe_av = p.v("PE")
        A("ac", (lambda cb=cb, hh=hh, t=t: lambda e: nc.scalar.copy(
            qfm[hh][:, t * 128:(t + 1) * 128], pb[cb][:, 256:384]))(),
          waits=[("PE", pe_av)], inc=("AC", 1))
        dv_avevac[it] = ("AC", p.v("AC"))

    pc_state = {"j": 0, "hist": {0: [], 1: []}}

    def pc_chunk():
        j = pc_state["j"]
        if j >= 128:
            return
        pc_state["j"] = j + 1
        bi = j % 2
        srcap = w1h if j < 64 else w2
        dstap = w1bf_dram if j < 64 else w2bf_dram
        col = (j % 64) * 2048
        sts = "PB%d" % bi
        hist = pc_state["hist"][bi]
        wv = [(sts, hist[-16])] if len(hist) >= 16 else []
        A("gp", (lambda srcap=srcap, dstap=dstap, col=col: lambda e:
                 e.dma_start(out=dstap[:, col:col + 2048],
                             in_=srcap[:, col:col + 2048]))(),
          waits=wv, inc=(sts, 16))
        hist.append(p.v(sts))

    _t1 = rope_q_t1(0)
    rope_q_pla(0, _t1)
    _plr = rope_q_plb(0)
    rope_q_add(0, _plr)
    rope_next = (rope_q_fin(0), _plr)
    for hh in range(16):
        dv_qr = rope_next
        kv = hh // 4
        for t in range(16):
            it = hh * 16 + t
            p5_front(it, hh, t, kv, dv_qr)
            if hh < 15:
                if t == 2:
                    _t1 = rope_q_t1(hh + 1)
                elif t == 3:
                    rope_q_pla(hh + 1, _t1)
                elif t == 4:
                    _plr = rope_q_plb(hh + 1)
                elif t == 7:
                    rope_q_add(hh + 1, _plr)
                elif t == 8:
                    rope_next = (rope_q_fin(hh + 1), _plr)
            if t % 2 == 0:
                pc_chunk()
            if it >= 4:
                p5_back1(it - 4)
            if it >= 5:
                p5_back2(it - 5)
    for j in range(252, 256):
        p5_back2(j - 1)
        p5_back1(j)
    p5_back2(255)
    PE_P5 = p.v("PE")
    DV_P5 = p.v("DV")
    AC_P5 = p.v("AC")

    pl_wocast = {}
    wo_last = {0: None, 1: None}
    wo_pb_last = {0: 0, 1: 0}
    pe_wo = {}
    dv_woevac = {}
    gd_wodma = {}
    for dm in range(4):
        for ch in range(16):
            sbi = (dm * 16 + ch) % 2
            wv = [("AC", pl_wocast.get(dm * 16 + ch - 2, 0))]
            if dm == 0 and ch < 2:
                wv += [("PE", PE_P5), ("DV", DV_P5), ("AC", AC_P5)]
            ws = "WS%d" % sbi
            A("sy", (lambda dm=dm, ch=ch, sbi=sbi: lambda e: e.dma_start(
                out=wst_f[sbi][:, :],
                in_=woT[ch * 128:(ch + 1) * 128, dm * 512:(dm + 1) * 512]))(),
              waits=wv, inc=(ws, 16))
            lwv = p.v(ws)
            wv = [(ws, lwv)]
            if dm < 2 and ch == 0:
                wv += [("PE", PE_P5), ("DV", DV_P5), ("AC", AC_P5)]
            else:
                wv.append(("PE", pe_wo.get((dm - 2, 15), 0)))
            A("ac", (lambda dm=dm, ch=ch, sbi=sbi: lambda e: nc.scalar.copy(
                wobf[dm % 2][:, ch, :], wst_f[sbi][:, :]))(),
              waits=wv, inc=("AC", 1))
            pl_wocast[dm * 16 + ch] = p.v("AC")
        plw = p.v("AC")
        for t in range(16):
            for ch in range(16):
                first = (ch == 0)
                wv = [("AC", plw)]
                if first:
                    wv.append(("DV", wo_pb_last[t % 2]))
                    if dm == 0:
                        wv.append(dv_avevac[15 * 16 + t])
                        if t < 2:
                            wv.append(("AC", AC_P5))
                A("pe", (lambda dm=dm, t=t, ch=ch, first=first:
                         lambda e: nc.tensor.matmul(
                             pb[t % 2][:, :], qfm[ch][:, t * 128:(t + 1) * 128],
                             wobf[dm % 2][:, ch, :], start=first, stop=(ch == 15)))(),
                  waits=wv, inc=("PE", 1) if ch == 15 else None)
            pe_wo[(dm, t)] = p.v("PE")
            A("dv", (lambda dm=dm, t=t: lambda e: nc.vector.tensor_copy(
                wos1[t % 2][:, :], pb[t % 2][:, :]))(),
              waits=[("PE", pe_wo[(dm, t)])] + (
                  [wo_last[t % 2]] if wo_last[t % 2] else []),
              inc=("DV", 1))
            dv_woevac[(dm, t)] = p.v("DV")
            wo_pb_last[t % 2] = p.v("DV")
            wos = "WO%d" % (t % 2)
            A("gp", (lambda dm=dm, t=t: lambda e: e.dma_start(
                out=wo_dram[t * 128:(t + 1) * 128, dm * 512:(dm + 1) * 512],
                in_=wos1[t % 2][:, :]))(),
              waits=[("DV", dv_woevac[(dm, t)])], inc=(wos, 16))
            gd_wodma[(dm, t)] = (wos, p.v(wos))
            wo_last[t % 2] = (wos, p.v(wos))
    GD_WO = [("WO0", p.v("WO0")), ("WO1", p.v("WO1"))]
    PE_WOD = p.v("PE")

    A("sy", lambda e: e.dma_start(out=wr_rows[:, :], in_=wr_row),
      waits=[("PE", PE_P5), ("DV", DV_P5)], inc=("WR", 16))
    LW_WR = p.v("WR")
    A("dv", lambda e: nc.vector.reduce_sum(sb12[0:1, 0:1], wr_rows[:, :], AX.X),
      waits=[("WR", LW_WR)], inc=("DV", 1))
    A("dv", lambda e: nc.vector.tensor_copy(sb12[0:1, 1:2], brs[:, :]), inc=("DV", 1))
    dv_s12 = p.v("DV")
    for qq in range(4):
        A("pe", (lambda qq=qq: lambda e: nc.tensor.matmul(
            pb[6][:, :], ones_f[:, :], wr_rows[0:1, qq * 512:(qq + 1) * 512],
            start=True, stop=True))(),
          waits=[("PE", PE_P5), ("DV", p.v("DV"))], inc=("PE", 1))
        pev = p.v("PE")
        A("dv", (lambda qq=qq: lambda e: nc.vector.tensor_copy(
            wr_bcast[:, qq * 512:(qq + 1) * 512], pb[6][:, :]))(),
          waits=[("PE", pev)], inc=("DV", 1))
    A("pe", lambda e: nc.tensor.matmul(
        pb[6][:, 0:2], ones_f[:, :], sb12[:, :], start=True, stop=True),
      waits=[("DV", dv_s12), ("DV", p.v("DV"))], inc=("PE", 1))
    pev = p.v("PE")
    A("dv", lambda e: nc.vector.tensor_copy(sigbr[:, :], pb[6][:, 0:2]),
      waits=[("PE", pev)], inc=("DV", 1))
    DV_SIG = p.v("DV")

    gd_x1w = {}
    ac_s2 = {}
    gd_wold = {}
    for t in range(16):
        sb = t % 2
        wv = list(GD_WO) + [("AC", ac_s2.get(t - 2, 0))]
        A("gp", (lambda t=t, sb=sb: lambda e: e.dma_start(
            out=wosc2[sb][:, :], in_=wo_dram[t * 128:(t + 1) * 128, :]))(),
          waits=wv, inc=("RL%d" % sb, 16))
        gd_wold[t] = ("RL%d" % sb, p.v("RL%d" % sb))
        x0s = "X0%d" % sb
        A("sy", (lambda t=t, sb=sb: lambda e: e.dma_start(
            out=x0st[sb][:, :], in_=x_in[HALO + t * 128:HALO + (t + 1) * 128, :]))(),
          waits=[("AC", ac_s2.get(t - 2, 0)),
                 ("PE", PE_WOD if t < 2 else 0), ("DV", DV_SIG if t < 2 else 0)],
          inc=(x0s, 16))
        lwv = p.v(x0s)
        A("dv", (lambda sb=sb: lambda e: nc.vector.reduce_sum(
            tsum[sb][:, :], wosc2[sb][:, :], AX.X, negate=True))(),
          waits=[gd_wold[t]], inc=("DV", 1))
        A("dv", (lambda sb=sb: lambda e: nc.vector.tensor_scalar(
            tsum[sb][:, :], tsum[sb][:, :], 1.0 / D, None, ALU.mult))(), inc=("DV", 1))
        dvnm = p.v("DV")
        A("ac", (lambda sb=sb: lambda e: nc.scalar.activation(
            junk[:, :], wosc2[sb][:, :], AF.Square, bias=tsum[sb][:, :],
            accum_out=tssq[sb][:, :]))(),
          waits=[("DV", dvnm)], inc=("AC", 1))
        acsq = p.v("AC")
        A("dv", (lambda sb=sb: lambda e: nc.vector.tensor_scalar(
            trstd[sb][:, :], tssq[sb][:, :], 1.0 / D, EPS, ALU.mult, ALU.add))(),
          waits=[("AC", acsq)], inc=("DV", 1))
        dvv = p.v("DV")
        A("ac", (lambda sb=sb: lambda e: nc.scalar.activation(
            trstd[sb][:, :], trstd[sb][:, :], AF.Ln))(),
          waits=[("DV", dvv)], inc=("AC", 1))
        A("ac", (lambda sb=sb: lambda e: nc.scalar.activation(
            trstd[sb][:, :], trstd[sb][:, :], AF.Exp, scale=-0.5))(), inc=("AC", 1))
        acr = p.v("AC")
        A("dv", (lambda sb=sb: lambda e: nc.vector.tensor_mul(
            tbias[sb][:, :], tsum[sb][:, :], trstd[sb][:, :]))(),
          waits=[("AC", acr)], inc=("DV", 1))
        dvb = p.v("DV")
        A("ac", (lambda sb=sb: lambda e: nc.scalar.activation(
            tmpa[sb][:, :], wosc2[sb][:, :], AF.Identity, bias=tbias[sb][:, :],
            scale=trstd[sb][:, :]))(),
          waits=[("DV", dvb)], inc=("AC", 1))
        aca = p.v("AC")
        A("dv", (lambda sb=sb: lambda e: nc.vector.tensor_add(
            x1bst[sb][:, :], tmpa[sb][:, :], x0st[sb][:, :]))(),
          waits=[("AC", aca), (x0s, lwv)] + (
              [gd_x1w[t - 2]] if (t - 2) in gd_x1w else []),
          inc=("DV", 1))
        dvx1 = p.v("DV")
        A("gp", (lambda t=t, sb=sb: lambda e: e.dma_start(
            out=x1_dram[t * 128:(t + 1) * 128, :], in_=x1bst[sb][:, :]))(),
          waits=[("DV", dvx1)], inc=("XS%d" % sb, 16))
        gd_x1w[t] = ("XS%d" % sb, p.v("XS%d" % sb))
        A("dv", (lambda sb=sb: lambda e: nc.vector.tensor_mul(
            junk2[:, :], x1bst[sb][:, :], wr_bcast[:, :]))(),
          waits=[("DV", DV_SIG)], inc=("DV", 1))
        A("dv", (lambda sb=sb: lambda e: nc.vector.reduce_sum(
            svec[sb][:, :], junk2[:, :], AX.X))(), inc=("DV", 1))
        A("dv", (lambda sb=sb, t=t: lambda e: nc.vector.reduce_sum(
            nm2s[:, t:t + 1], x1bst[sb][:, :], AX.X, negate=True))(), inc=("DV", 1))
        A("dv", (lambda t=t: lambda e: nc.vector.tensor_scalar(
            nm2s[:, t:t + 1], nm2s[:, t:t + 1], 1.0 / D, None, ALU.mult))(),
          inc=("DV", 1))
        dvn2 = p.v("DV")
        A("ac", (lambda sb=sb, t=t: lambda e: nc.scalar.activation(
            junk[:, :], x1bst[sb][:, :], AF.Square, bias=nm2s[:, t:t + 1],
            accum_out=tssq[sb][:, :]))(),
          waits=[("DV", dvn2)], inc=("AC", 1))
        ac_s2[t] = p.v("AC")
        A("dv", (lambda sb=sb, t=t: lambda e: nc.vector.tensor_scalar(
            r2s[:, t:t + 1], tssq[sb][:, :], 1.0 / D, EPS, ALU.mult, ALU.add))(),
          waits=[("AC", ac_s2[t])], inc=("DV", 1))
        dvv2 = p.v("DV")
        A("ac", (lambda t=t: lambda e: nc.scalar.activation(
            r2s[:, t:t + 1], r2s[:, t:t + 1], AF.Ln))(),
          waits=[("DV", dvv2)], inc=("AC", 1))
        A("ac", (lambda t=t: lambda e: nc.scalar.activation(
            r2s[:, t:t + 1], r2s[:, t:t + 1], AF.Exp, scale=-0.5))(), inc=("AC", 1))
        acr2 = p.v("AC")
        A("dv", (lambda sb=sb, t=t: lambda e: nc.vector.scalar_tensor_tensor(
            svec[sb][:, :], nm2s[:, t:t + 1], sigbr[:, 0:1], svec[sb][:, :],
            ALU.mult, ALU.add))(),
          waits=[("AC", acr2)], inc=("DV", 1))
        A("dv", (lambda sb=sb, t=t: lambda e: nc.vector.tensor_mul(
            svec[sb][:, :], svec[sb][:, :], r2s[:, t:t + 1]))(), inc=("DV", 1))
        A("dv", (lambda sb=sb, t=t: lambda e: nc.vector.tensor_add(
            logits[:, t:t + 1], svec[sb][:, :], sigbr[:, 1:2]))(), inc=("DV", 1))
    DV_A3 = p.v("DV")
    GD_X1 = [("XS0", p.v("XS0")), ("XS1", p.v("XS1"))]
    AC_A3 = p.v("AC")

    A("gp", lambda e: e.dma_start(
        out=log_db.rearrange("o (pp t) -> (o pp) t", t=16), in_=logits[:, :]),
      waits=[("DV", DV_A3)], inc=("LG", 16))
    gdl = p.v("LG")
    A("gp", lambda e: nc.gpsimd.collective_compute(
        "AllGather", ALU.bypass,
        replica_groups=[[0, 1], [2, 3], [4, 5], [6, 7]],
        ins=[log_db.opt()], outs=[log_gth.opt()]),
      waits=[("LG", gdl)], inc=("C0", 1))
    for r in range(2):
        A("gp", (lambda r=r: lambda e: e.dma_start(
            out=Lall[:, r * 16:(r + 1) * 16],
            in_=log_gth[r:r + 1, :].rearrange("o (pp t) -> (o pp) t", t=16)))(),
          waits=[("C0", 1)], inc=("LA", 16))
    GD_LALL = p.v("LA")
    A("dv", lambda e: nc.vector.reduce_max(colr[:, :], Lall[:, :], AX.X),
      waits=[("LA", GD_LALL)], inc=("DV", 1))
    dv1 = p.v("DV")
    A("pe", lambda e: nc.tensor.transpose(pb[6][0:1, 0:128], colr[:, :], identF[:, :]),
      waits=[("DV", dv1), ("DV", DV_A3)], inc=("PE", 1))
    pev = p.v("PE")
    A("dv", lambda e: nc.vector.tensor_copy(rowr[:, :], pb[6][0:1, 0:128]),
      waits=[("PE", pev)], inc=("DV", 1))
    A("dv", lambda e: nc.vector.reduce_max(gmax[:, :], rowr[:, :], AX.X), inc=("DV", 1))
    A("dv", lambda e: nc.vector.tensor_scalar(hi[:, :], gmax[:, :], 1.0, None, ALU.add),
      inc=("DV", 1))
    A("dv", lambda e: nc.vector.tensor_scalar(cntg[:, :], Lall[:, :], -1.0, None,
                                              ALU.mult), inc=("DV", 1))
    A("dv", lambda e: nc.vector.reduce_max(colr[:, :], cntg[:, :], AX.X), inc=("DV", 1))
    dv1 = p.v("DV")
    A("pe", lambda e: nc.tensor.transpose(pb[6][0:1, 0:128], colr[:, :], identF[:, :]),
      waits=[("DV", dv1)], inc=("PE", 1))
    pev = p.v("PE")
    A("dv", lambda e: nc.vector.tensor_copy(rowr[:, :], pb[6][0:1, 0:128]),
      waits=[("PE", pev)], inc=("DV", 1))
    A("dv", lambda e: nc.vector.reduce_max(lo[:, :], rowr[:, :], AX.X), inc=("DV", 1))
    A("dv", lambda e: nc.vector.tensor_scalar(lo[:, :], lo[:, :], -1.0, -1.0,
                                              ALU.mult, ALU.add), inc=("DV", 1))
    for it in range(BISECT):
        A("dv", lambda e: nc.vector.tensor_add(mid[:, :], lo[:, :], hi[:, :]),
          inc=("DV", 1))
        A("dv", lambda e: nc.vector.tensor_scalar(mid[:, :], mid[:, :], 0.5, None,
                                                  ALU.mult), inc=("DV", 1))
        dvm = p.v("DV")
        A("pe", lambda e: nc.tensor.matmul(pb[6][:, 0:1], ones_f[:, :], mid[:, :],
                                           start=True, stop=True),
          waits=[("DV", dvm)], inc=("PE", 1))
        pev = p.v("PE")
        A("dv", lambda e: nc.vector.tensor_copy(thrb[:, :], pb[6][:, 0:1]),
          waits=[("PE", pev)], inc=("DV", 1))
        A("dv", lambda e: nc.vector.tensor_scalar(
            cntg[:, :], Lall[:, :], thrb[:, :], 0.0, ALU.is_gt, ALU.add,
            accum_out=cnt[:, :]), inc=("DV", 1))
        dvc = p.v("DV")
        A("pe", lambda e: nc.tensor.matmul(pb[7][0:1, 0:1], cnt[:, :], onescol[:, :],
                                           start=True, stop=True),
          waits=[("DV", dvc)], inc=("PE", 1))
        pev = p.v("PE")
        A("dv", lambda e: nc.vector.tensor_scalar(flag[:, :], pb[7][0:1, 0:1], 511.5,
                                                  None, ALU.is_gt),
          waits=[("PE", pev)], inc=("DV", 1))
        A("dv", lambda e: nc.vector.tensor_sub(dds[:, :], mid[:, :], lo[:, :]),
          inc=("DV", 1))
        A("dv", lambda e: nc.vector.tensor_sub(ees[:, :], hi[:, :], mid[:, :]),
          inc=("DV", 1))
        A("dv", lambda e: nc.vector.scalar_tensor_tensor(
            lo[:, :], flag[:, :], dds[:, :], lo[:, :], ALU.mult, ALU.add),
          inc=("DV", 1))
        A("dv", lambda e: nc.vector.scalar_tensor_tensor(
            hi[:, :], flag[:, :], ees[:, :], mid[:, :], ALU.mult, ALU.add),
          inc=("DV", 1))
    dvm = p.v("DV")
    A("pe", lambda e: nc.tensor.matmul(pb[6][:, 0:1], ones_f[:, :], hi[:, :],
                                       start=True, stop=True),
      waits=[("DV", dvm)], inc=("PE", 1))
    pev = p.v("PE")
    A("dv", lambda e: nc.vector.tensor_copy(thrb[:, :], pb[6][:, 0:1]),
      waits=[("PE", pev)], inc=("DV", 1))
    A("dv", lambda e: nc.vector.tensor_scalar(
        mfull[:, :], Lall[:, :], thrb[:, :], 0.0, ALU.is_gt, ALU.add,
        accum_out=totp[:, :]), inc=("DV", 1))
    A("dv", lambda e: nc.vector.tensor_scalar(
        mown[:, :], logits[:, :], thrb[:, :], 0.0, ALU.is_gt, ALU.add),
      inc=("DV", 1))
    A("dv", lambda e: nc.vector.tensor_scalar(
        umown[:, :], mown[:, :], -1.0, 1.0, ALU.mult, ALU.add), inc=("DV", 1))
    A("dv", lambda e: nc.vector.reduce_sum(tot0[:, :], mfull[:, 0:16], AX.X),
      inc=("DV", 1))
    A("dv", lambda e: nc.vector.tensor_tensor_scan(
        scanf[:, :], mown[:, :], zer16[:, :], 0.0, ALU.add, ALU.add),
      waits=[("DV", DV_ZERO)], inc=("DV", 1))
    dvs = p.v("DV")
    A("pe", lambda e: nc.tensor.matmul(pb[6][:, 0:1], lmask[:, :], totp[:, :],
                                       start=True, stop=True),
      waits=[("DV", dvs)], inc=("PE", 1))
    pev = p.v("PE")
    A("dv", lambda e: nc.vector.tensor_copy(offs[:, :], pb[6][:, 0:1]),
      waits=[("PE", pev)], inc=("DV", 1))
    A("dv", lambda e: nc.vector.tensor_scalar(
        slotF[:, :], scanf[:, :], offs[:, :], -1.0, ALU.add, ALU.add), inc=("DV", 1))
    A("dv", lambda e: nc.vector.tensor_mul(colr[:, :], myrank[:, :], tot0[:, :]),
      inc=("DV", 1))
    A("dv", lambda e: nc.vector.tensor_scalar(
        slotF[:, :], slotF[:, :], colr[:, :], None, ALU.add), inc=("DV", 1))
    A("dv", lambda e: nc.vector.tensor_scalar(
        cntg[:, 0:16], umown[:, :], BIG, None, ALU.mult), inc=("DV", 1))
    A("dv", lambda e: nc.vector.tensor_add(slotF[:, :], slotF[:, :], cntg[:, 0:16]),
      inc=("DV", 1))
    A("dv", lambda e: nc.vector.tensor_copy(slotI[:, :], slotF[:, :]), inc=("DV", 1))
    DV_SLOT = p.v("DV")

    for j in range(16):
        A("gp", (lambda j=j: lambda e: e.dma_start(
            out=xg_dram.rearrange("a b -> (a b)")[j * 65536:(j + 1) * 65536]
            .rearrange("(pp n) -> pp n", pp=128),
            in_=zerot[:, :]))(),
          waits=[("DV", DV_ZERO)] + (list(GD_X1) if j == 0 else []), inc=("LZ", 16))
    GD_XGZ = p.v("LZ")
    gd_sc = {}
    ac_x2 = {}
    for t in range(16):
        sb = t % 2
        xbs = "XB%d" % sb
        A("gp", (lambda t=t, sb=sb: lambda e: e.dma_start(
            out=x1bt[sb][:, :], in_=x1_dram[t * 128:(t + 1) * 128, :]))(),
          waits=list(GD_X1) + [("AC", ac_x2.get(t - 2, 0)),
                               ("PE", PE_WOD if t < 2 else 0)],
          inc=(xbs, 16))
        gdv = (xbs, p.v(xbs))
        A("dv", (lambda sb=sb, t=t: lambda e: nc.vector.tensor_mul(
            tbias[sb][:, :], nm2s[:, t:t + 1], r2s[:, t:t + 1]))(),
          waits=[("DV", DV_SLOT), ("AC", ac_x2.get(t - 2, 0))], inc=("DV", 1))
        dvb = p.v("DV")
        A("ac", (lambda sb=sb, t=t: lambda e: nc.scalar.activation(
            xn2scr[sb][:, :], x1bt[sb][:, :], AF.Identity, bias=tbias[sb][:, :],
            scale=r2s[:, t:t + 1]))(),
          waits=[("DV", dvb), gdv, ("AC", AC_A3 if t < 2 else 0)] + (
              [gd_sc[t - 2]] if (t - 2) in gd_sc else []),
          inc=("AC", 1))
        ac_x2[t] = p.v("AC")
        A("gp", (lambda t=t, sb=sb: lambda e: nc.gpsimd.indirect_dma_start(
            out=xg_dram, out_offset=IndirectOffsetOnAxis(slotI[:, t:t + 1], 0),
            in_=xn2scr[sb][:, :], in_offset=None,
            bounds_check=CAP - 1, oob_is_err=False))(),
          waits=[("AC", ac_x2[t]), ("LZ", GD_XGZ)], inc=("SC%d" % sb, 16))
        gd_sc[t] = ("SC%d" % sb, p.v("SC%d" % sb))
    GD_SCAT = [("SC0", p.v("SC0")), ("SC1", p.v("SC1"))]
    A("gp", lambda e: nc.gpsimd.collective_compute(
        "AllReduce", ALU.add,
        replica_groups=[[0, 1], [2, 3], [4, 5], [6, 7]],
        ins=[xg_dram.opt()], outs=[xg_red.opt()]),
      waits=list(GD_SCAT), inc=("C1", 1))
    for j in range(2):
        A("gp", (lambda j=j: lambda e: nc.gpsimd.indirect_dma_start(
            out=xgtm[j][:, :], out_offset=None,
            in_=xg_red, in_offset=IndirectOffsetOnAxis(slotrow[:, j:j + 1], 0)))(),
          waits=[("C1", 1)], inc=("XG", 16))
    GD_XGL = p.v("XG")
    dv_xgbank = {6: 0, 7: 0}
    for j in range(2):
        for d in range(16):
            bk = 6 + (d % 2)
            A("pe", (lambda j=j, d=d, bk=bk: lambda e: nc.tensor.transpose(
                pb[bk][:, 0:128], xgtm[j][:, d * 128:(d + 1) * 128], identF[:, :]))(),
              waits=[("XG", GD_XGL), ("DV", dv_xgbank[bk])],
              inc=("PE", 1))
            pev = p.v("PE")
            A("dv", (lambda j=j, d=d, bk=bk: lambda e: nc.vector.tensor_copy(
                xgT[d][:, j * 128:(j + 1) * 128], pb[bk][:, 0:128]))(),
              waits=[("PE", pev)], inc=("DV", 1))
            dv_xgbank[bk] = p.v("DV")
    DV_XGT = p.v("DV")

    ac_gelu = {}
    pe_w1 = {}
    dv_w1c = {}
    for fq in range(16):
        for g in range(4):
            k = fq * 4 + g
            sbi = k % 2
            w1s = "W1%d" % sbi
            wv = [("PE", pe_w1.get(k - 2, 0))]
            if k < 2:
                wv += [("DV", DV_XGT), ("PB0", p.v("PB0")), ("PB1", p.v("PB1"))]
            A("sy", (lambda fq=fq, g=g, sbi=sbi: lambda e: e.dma_start(
                out=w1st[sbi][:, :],
                in_=w1bf_dram[:, (fq * 16 + g * 4) * 512:(fq * 16 + g * 4 + 4) * 512]))(),
              waits=wv, inc=(w1s, 16))
            lwv = p.v(w1s)
            for c in range(4):
                d = g * 4 + c
                for j in range(4):
                    wv = [(w1s, lwv), ("DV", DV_XGT)]
                    if d == 0:
                        wv.append(("AC", ac_gelu.get(fq - 1, 0)))
                    A("pe", (lambda sbi=sbi, c=c, j=j, d=d: lambda e: nc.tensor.matmul(
                        pb[j][:, 0:HALFCAP],
                        w1st[sbi][:, (c * 4 + j) * 128:(c * 4 + j + 1) * 128],
                        xgT[d][:, :], start=(d == 0), stop=(d == 15)))(),
                      waits=wv,
                      inc=("PE", 1) if j == 3 else None)
            pe_w1[k] = p.v("PE")
        pew = p.v("PE")
        for j in range(4):
            f = fq * 4 + j
            A("ac", (lambda f=f, j=j: lambda e: nc.scalar.activation(
                h_bf[f][:, :], pb[j][:, 0:HALFCAP], AF.Gelu_apprx_tanh,
                bias=b1rs[:, f:f + 1]))(),
              waits=[("PE", pew)], inc=("AC", 1))
        ac_gelu[fq] = p.v("AC")
    AC_W1 = p.v("AC")
    A("sy", lambda e: e.dma_start(out=b2rows[:, :], in_=b2row),
      waits=[("PE", PE_WOD)], inc=("B2", 16))
    LW_B2 = p.v("B2")
    A("dv", lambda e: nc.vector.tensor_copy(b2row_bf[:, :], b2rows[:, :]),
      waits=[("B2", LW_B2)], inc=("DV", 1))
    PL_B2 = p.v("DV")

    dv_yevac = {}
    gd_y = {}
    pl_w2 = {}
    pe_w2 = {}
    for dm in range(4):
        for g in range(16):
            k = dm * 16 + g
            sbi = k % 2
            w2s = "W2%d" % sbi
            A("sy", (lambda dm=dm, g=g, sbi=sbi: lambda e: e.dma_start(
                out=w2st[sbi][:, :],
                in_=w2bf_dram[:, (dm * 64 + g * 4) * 512:(dm * 64 + g * 4 + 4) * 512]))(),
              waits=[("PE", pe_w2.get(k - 2, 0))], inc=(w2s, 16))
            lwv = p.v(w2s)
            for j in range(4):
                f = g * 4 + j
                for sc in range(2):
                    bk = (dm % 2) * 2 + sc
                    wv = [(w2s, lwv), ("AC", AC_W1)]
                    if f == 0:
                        wv.append(("DV", dv_yevac.get((dm - 2, sc), 0)))
                    A("pe", (lambda f=f, j=j, sc=sc, bk=bk, sbi=sbi:
                             lambda e: nc.tensor.matmul(
                                 pb[bk][:, :], h_bf[f][:, sc * 128:(sc + 1) * 128],
                                 w2st[sbi][:, j * 512:(j + 1) * 512],
                                 start=(f == 0), stop=False))(),
                      waits=wv, inc=("PE", 1) if sc == 1 else None)
            pe_w2[k] = p.v("PE")
        for sc in range(2):
            bk = (dm % 2) * 2 + sc
            A("pe", (lambda dm=dm, bk=bk: lambda e: nc.tensor.matmul(
                pb[bk][:, :], ones_bf[:, :], b2row_bf[0:1, dm * 512:(dm + 1) * 512],
                start=False, stop=True))(),
              waits=[("DV", PL_B2)], inc=("PE", 1))
            pev = p.v("PE")
            A("dv", (lambda sc=sc, bk=bk: lambda e: nc.vector.tensor_copy(
                y_sb[sc][:, :], pb[bk][:, :]))(),
              waits=[("PE", pev)] + (
                  [gd_y[(dm - 1, sc)]] if (dm - 1, sc) in gd_y else []),
              inc=("DV", 1))
            dv_yevac[(dm, sc)] = p.v("DV")
            A("gp", (lambda dm=dm, sc=sc: lambda e: e.dma_start(
                out=ymine[sc * 128:(sc + 1) * 128, dm * 512:(dm + 1) * 512],
                in_=y_sb[sc][:, :]))(),
              waits=[("DV", dv_yevac[(dm, sc)])], inc=("YS%d" % sc, 16))
            gd_y[(dm, sc)] = ("YS%d" % sc, p.v("YS%d" % sc))
    GD_Y = [("YS0", p.v("YS0")), ("YS1", p.v("YS1"))]
    A("gp", lambda e: nc.gpsimd.collective_compute(
        "AllGather", ALU.bypass,
        replica_groups=[[0, 1], [2, 3], [4, 5], [6, 7]],
        ins=[ymine.opt()], outs=[y_full.opt()]),
      waits=list(GD_Y), inc=("C2", 1))

    gd_out = {}
    gd_xb = {}
    gd_yb = {}
    dv_tb = {}
    dv_s2 = {}
    dv_rs = {}
    dv_ybfree = {}
    ac_s1 = {}
    ac_sq = {}
    ac_apply = {}
    pl_add = {}

    def fin_s0(t):
        xb = "XB%d" % (t % 6)
        wv = [("C2", 1)]
        if (t - 6) in pl_add:
            wv.append(("PL", pl_add[t - 6]))
        A("gp", (lambda t=t: lambda e: e.dma_start(
            out=x1bt[t % 6][:, :], in_=x1_dram[t * 128:(t + 1) * 128, :]))(),
          waits=wv, inc=(xb, 16))
        gd_xb[t] = (xb, p.v(xb))
        yb = "YB%d" % (t % 4)
        wv = [("C2", 1)]
        if (t - 4) in dv_ybfree:
            wv.append(("DV", dv_ybfree[t - 4]))
        A("gp", (lambda t=t: lambda e: nc.gpsimd.indirect_dma_start(
            out=ybuf[t % 4][:, :], out_offset=None,
            in_=y_full, in_offset=IndirectOffsetOnAxis(slotI[:, t:t + 1], 0),
            bounds_check=CAP - 1, oob_is_err=False))(),
          waits=wv, inc=(yb, 16))
        gd_yb[t] = (yb, p.v(yb))
        wv = []
        if (t - 4) in ac_s1:
            wv.append(("AC", ac_s1[t - 4]))
        A("dv", (lambda t=t: lambda e: nc.vector.tensor_mul(
            tbias3[t % 4][:, :], nm2s[:, t:t + 1], r2s[:, t:t + 1]))(),
          waits=wv, inc=("DV", 1))
        dv_tb[t] = p.v("DV")

    def fin_s1(t):
        wv = [("DV", dv_tb[t]), gd_xb[t]]
        if t < 2:
            wv.append(("AC", AC_W1))
        if (t - 4) in dv_s2:
            wv.append(("DV", dv_s2[t - 4]))
        A("ac", (lambda t=t: lambda e: nc.scalar.activation(
            xn2scr[t % 4][:, :], x1bt[t % 6][:, :], AF.Identity,
            bias=tbias3[t % 4][:, :], scale=r2s[:, t:t + 1]))(),
          waits=wv, inc=("AC", 1))
        ac_s1[t] = p.v("AC")

    def fin_s2(t):
        A("dv", (lambda t=t: lambda e: nc.vector.tensor_mul(
            gmcol[t % 2][:, :], mown[:, t:t + 1], logits[:, t:t + 1]))(),
          inc=("DV", 1))
        A("dv", (lambda t=t: lambda e: nc.vector.tensor_scalar(
            junk2[:, :], ybuf[t % 4][:, :], gmcol[t % 2][:, :], None,
            ALU.mult))(),
          waits=[gd_yb[t]], inc=("DV", 1))
        dv_ybfree[t] = p.v("DV")
        A("dv", (lambda t=t: lambda e: nc.vector.scalar_tensor_tensor(
            xn2scr[t % 4][:, :], xn2scr[t % 4][:, :], umown[:, t:t + 1],
            junk2[:, :], ALU.mult, ALU.add))(),
          waits=[("AC", ac_s1[t])], inc=("DV", 1))
        A("dv", (lambda t=t: lambda e: nc.vector.reduce_sum(
            tsum3[t % 4][:, :], xn2scr[t % 4][:, :], AX.X, negate=True))(),
          inc=("DV", 1))
        A("dv", (lambda t=t: lambda e: nc.vector.tensor_scalar(
            tsum3[t % 4][:, :], tsum3[t % 4][:, :], 1.0 / D, None,
            ALU.mult))(), inc=("DV", 1))
        dv_s2[t] = p.v("DV")

    def fin_s3a(t):
        A("ac", (lambda t=t: lambda e: nc.scalar.activation(
            junk[:, :], xn2scr[t % 4][:, :], AF.Square,
            bias=tsum3[t % 4][:, :], accum_out=tssq3[t % 4][:, :]))(),
          waits=[("DV", dv_s2[t]), ("DV", dv_rs.get(t - 4, 0))],
          inc=("AC", 1))
        ac_sq[t] = p.v("AC")

    def fin_s3b(t):
        A("dv", (lambda t=t: lambda e: nc.vector.tensor_scalar(
            trstd3[t % 4][:, :], tssq3[t % 4][:, :], 1.0 / D, EPS,
            ALU.mult, ALU.add))(),
          waits=[("AC", ac_sq[t]), ("AC", ac_apply.get(t - 4, 0))],
          inc=("DV", 1))
        dv_rs[t] = p.v("DV")

    def fin_s4(t):
        A("ac", (lambda t=t: lambda e: nc.scalar.activation(
            trstd3[t % 4][:, :], trstd3[t % 4][:, :], AF.Ln))(),
          waits=[("DV", dv_rs[t])], inc=("AC", 1))
        A("ac", (lambda t=t: lambda e: nc.scalar.activation(
            trstd3[t % 4][:, :], trstd3[t % 4][:, :], AF.Exp, scale=-0.5))(),
          inc=("AC", 1))
        A("ac", (lambda t=t: lambda e: nc.scalar.activation(
            tbias3b[t % 2][:, :], tsum3[t % 4][:, :], AF.Identity,
            scale=trstd3[t % 4][:, :]))(), inc=("AC", 1))
        wv = []
        if (t - 2) in pl_add:
            wv.append(("PL", pl_add[t - 2]))
        A("ac", (lambda t=t: lambda e: nc.scalar.activation(
            yout[t % 2][:, :], xn2scr[t % 4][:, :], AF.Identity,
            bias=tbias3b[t % 2][:, :], scale=trstd3[t % 4][:, :]))(),
          waits=wv, inc=("AC", 1))
        ac_apply[t] = p.v("AC")

    def fin_s5(t):
        wv = [("AC", ac_apply[t])]
        if (t - 2) in gd_out:
            wv.append(gd_out[t - 2])
        A("pl", (lambda t=t: lambda e: nc.gpsimd.tensor_add(
            outst[t % 2][:, :], yout[t % 2][:, :], x1bt[t % 6][:, :]))(),
          waits=wv, inc=("PL", 1))
        pl_add[t] = p.v("PL")
        A("gp", (lambda t=t: lambda e: e.dma_start(
            out=out[t * 128:(t + 1) * 128, :], in_=outst[t % 2][:, :]))(),
          waits=[("PL", pl_add[t])], inc=("OU%d" % (t % 2), 16))
        gd_out[t] = ("OU%d" % (t % 2), p.v("OU%d" % (t % 2)))

    for s in range(21):
        if s < 16:
            fin_s0(s)
        if 1 <= s <= 16:
            fin_s1(s - 1)
        if 2 <= s <= 17:
            fin_s2(s - 2)
        if 3 <= s <= 18:
            fin_s3a(s - 3)
        if 4 <= s <= 19:
            fin_s4(s - 4)
        if 3 <= s <= 18:
            fin_s3b(s - 3)
        if 5 <= s <= 20:
            fin_s5(s - 5)
    A("gp", lambda e: nc.gpsimd.memset(zerot[0:1, 0:1], 0.0),
      waits=[("OU0", p.v("OU0")), ("OU1", p.v("OU1"))], inc=None)

    semstack = contextlib.ExitStack()
    with nc.Block() as block:
        sems = {n: semstack.enter_context(nc.semaphore("sem_" + n))
                for n in p.cnt}

        @block.sync
        def _(eng):
            p.play("sy", eng, sems)

        @block.tensor
        def _(eng):
            p.play("pe", eng, sems)

        @block.vector
        def _(eng):
            p.play("dv", eng, sems)

        @block.scalar
        def _(eng):
            p.play("ac", eng, sems)

        @block.gpsimd
        def _(eng):
            p.play(("gp", "pl"), eng, sems)

    semstack.close()
    stack.close()
    return nc


def _host_inputs(inputs):
    x = np.asarray(inputs["L0_attn_x"], dtype=np.float32)
    Wq = np.asarray(inputs["L0_attn_Wq"], dtype=np.float32)
    Wkv = np.asarray(inputs["L0_attn_Wkv"], dtype=np.float32)
    Wo = np.asarray(inputs["L0_attn_Wo"], dtype=np.float32)
    qk = np.asarray(inputs["L0_attn_qk_temp"], dtype=np.float32).reshape(16, 1)
    W_r = np.asarray(inputs["L0_ffn_W_r"], dtype=np.float32)
    b_r = np.asarray(inputs["L0_ffn_b_r"], dtype=np.float32).reshape(1, 1)
    W1 = np.asarray(inputs["L0_ffn_W1"], dtype=np.float32)
    b1 = np.asarray(inputs["L0_ffn_b1"], dtype=np.float32)
    W2 = np.asarray(inputs["L0_ffn_W2"], dtype=np.float32)
    b2 = np.asarray(inputs["L0_ffn_b2"], dtype=np.float32)

    bf = ml_dtypes.bfloat16
    wqT = np.ascontiguousarray(Wq.T)
    wkvT = np.ascontiguousarray(Wkv.T)
    woT = np.ascontiguousarray(Wo.T)
    w1h = np.ascontiguousarray(
        W1.reshape(16, 128, 16, 4, 128).transpose(1, 2, 0, 3, 4).reshape(128, -1))
    w2h = np.ascontiguousarray(
        W2.reshape(64, 128, 4, 512).transpose(1, 2, 0, 3).reshape(128, -1))
    b1r = np.ascontiguousarray(b1.reshape(64, 128).T)
    b2row = np.ascontiguousarray(b2.reshape(1, D))
    wr_row = np.ascontiguousarray(W_r.reshape(1, D))

    identB = np.eye(128, dtype=bf)
    identF = np.eye(128, dtype=np.float32)
    r = np.arange(128)
    maskA = np.where(r[None, :] > r[:, None], 0.0, NEG)
    maskB = np.where(r[:, None] >= r[None, :], 0.0, NEG)
    maskAB = np.concatenate(
        [maskA, np.zeros((128, 128)), maskB], axis=1).astype(bf)
    ones_bf = np.ones((1, 128), dtype=bf)
    ones_f = np.ones((1, 128), dtype=np.float32)
    onescol = np.ones((128, 1), dtype=np.float32)
    bigoh = np.zeros((128, 127), dtype=bf)
    bigoh[:, 63] = 1.0
    lmask = (r[:, None] < r[None, :]).astype(np.float32)
    selq = np.ascontiguousarray(np.kron(np.eye(16), np.ones((1, 128)))).astype(bf)

    inv_freq = 1.0 / (10000.0 ** (np.arange(0, DH, 2, dtype=np.float64) / DH))

    maps = []
    for c in range(8):
        b, hh = c // 2, c % 2
        t0 = hh * T
        xp = np.zeros((TK, D), dtype=np.float32)
        lo_ = max(0, t0 - HALO)
        xp[HALO - (t0 - lo_):HALO] = x[b, lo_:t0]
        xp[HALO:] = x[b, t0:t0 + T]
        pos = (t0 - HALO) + np.arange(TK, dtype=np.float64)
        ang = pos[None, :] * inv_freq[:, None]
        cosb = np.ascontiguousarray(
            np.concatenate([np.cos(ang), np.cos(ang)], axis=0).astype(bf))
        sinb = np.ascontiguousarray(
            np.concatenate([np.sin(ang), -np.sin(ang)], axis=0).astype(bf))
        kb = np.zeros((1, TK), dtype=bf)
        if hh == 0:
            kb[0, :HALO] = NEG
        myrank = np.full((128, 1), float(hh), dtype=np.float32)
        slotrow = np.ascontiguousarray(np.stack(
            [hh * 256 + np.arange(128), hh * 256 + 128 + np.arange(128)],
            axis=1).astype(np.int32))
        maps.append({
            "x_in": xp, "wqT": wqT, "wkvT": wkvT, "woT": woT, "w1h": w1h, "w2": w2h,
            "b1r": b1r, "b2row": b2row, "wr_row": wr_row, "br": b_r, "qk16": qk,
            "cosb": cosb, "sinb": sinb, "kbias": kb, "identB": identB,
            "identF": identF, "maskAB": maskAB, "ones_bf": ones_bf,
            "ones_f": ones_f, "onescol": onescol, "bigoh": bigoh, "lmask": lmask,
            "myrank": myrank, "slotrow": slotrow, "selq": selq,
        })
    return maps


def kernel(**inputs):
    maps = _host_inputs(inputs)
    nc = build_nc()
    res = run_bass_kernel_spmd(nc, maps, core_ids=list(range(8)))
    outp = np.empty((4, 4096, 2048), dtype=np.float32)
    for c in range(8):
        b, hh = c // 2, c % 2
        outp[b, hh * T:(hh + 1) * T] = np.asarray(res.results[c]["out"])
    return outp


# version 26
# speedup vs baseline: 1.2135x; 1.2135x over previous
import numpy as np
import ml_dtypes
import concourse.bass as bass
import concourse.mybir as mybir
from concourse.bass_utils import run_bass_kernel_spmd
from concourse.bass_core_rust import IndirectOffsetOnAxis

F32 = mybir.dt.float32
F32R = mybir.dt.float32r
BF16 = mybir.dt.bfloat16
I32 = mybir.dt.int32
U8 = mybir.dt.uint8
AF = mybir.ActivationFunctionType
ALU = mybir.AluOpType
AX = mybir.AxisListType

D = 2048
T = 2048
TK = 2304
HALO = 256
DH = 128
NT = 16
NTK = 18
DFF = 8192
NEG = -1.0e30
EPS = 1e-5
BIG = 1.0e6
CAP = 512
HALFCAP = 256
BISECT = 30


class Prog:

    def __init__(self):
        self.steps = []
        self.cnt = {}

    def sem(self, *names):
        for n in names:
            self.cnt.setdefault(n, 0)

    def v(self, s):
        return self.cnt[s]

    ENG_SEM = {"dv": "DV", "ac": "AC", "pl": "PL"}
    SELF_EDGES = {"dv", "ac", "pl"}

    def add(self, eng, emit, waits=(), inc=None):
        auto = self.ENG_SEM.get(eng)
        selfwait = None
        if auto is not None:
            if inc is None:
                inc = (auto, 1)
            assert inc[0] == auto and inc[1] == 1
            if eng in self.SELF_EDGES:
                selfwait = (auto, self.cnt[auto])
        self.steps.append((eng, tuple(waits), emit, inc, selfwait))
        if inc is not None:
            self.cnt[inc[0]] += inc[1]

    def play(self, name, eng, sems):
        names = name if isinstance(name, (set, tuple)) else {name}
        lastw = {}
        for engine, waits, emit, inc, selfwait in self.steps:
            if engine not in names:
                continue
            allw = list(waits)
            if selfwait is not None:
                allw.append(selfwait)
            for s, v in allw:
                if v > 0 and lastw.get(s, -1) < v:
                    eng.wait_ge(sems[s], v)
                    lastw[s] = v
            ins = emit(eng)
            if inc is not None:
                ins.then_inc(sems[inc[0]], inc[1])


def build_nc():
    nc = bass.Bass("TRN2", target_bir_lowering=False)

    def din(name, shape, dt=F32):
        return nc.dram_tensor(name, shape, dt, kind="ExternalInput").ap()

    x_in = din("x_in", [TK, D])
    wqT = din("wqT", [D, D])
    wkvT = din("wkvT", [D, 1024])
    woT = din("woT", [D, D])
    w1h = din("w1h", [128, 16 * 64 * 128])
    w2 = din("w2", [128, 4 * 64 * 512])
    b1r = din("b1r", [128, 64])
    b2row = din("b2row", [1, D])
    wr_row = din("wr_row", [1, D])
    br_in = din("br", [1, 1])
    qk16 = din("qk16", [16, 1])
    cosb_in = din("cosb", [128, TK], BF16)
    sinb_in = din("sinb", [128, TK], BF16)
    kbias_in = din("kbias", [1, TK], BF16)
    identB_in = din("identB", [128, 128], BF16)
    identF_in = din("identF", [128, 128], F32)
    maskAB_in = din("maskAB", [128, 384], BF16)
    ones_bf_in = din("ones_bf", [1, 128], BF16)
    ones_f_in = din("ones_f", [1, 128], F32)
    onescol_in = din("onescol", [128, 1], F32)
    bigoh_in = din("bigoh", [128, 127], BF16)
    lmask_in = din("lmask", [128, 128], F32)
    myrank_in = din("myrank", [128, 1], F32)
    slotrow_in = din("slotrow", [128, 2], I32)
    selq_in = din("selq", [16, D], BF16)

    out = nc.dram_tensor("out", [T, D], F32, kind="ExternalOutput").ap()

    wo_dram = nc.dram_tensor("wo_dram", [T, D], F32).ap()
    x1_dram = nc.dram_tensor("x1_dram", [T, D], BF16).ap()
    log_db = nc.dram_tensor("log_db", [1, T], F32).ap()
    log_gth = nc.dram_tensor("log_gth", [2, T], F32).ap()
    xg_dram = nc.dram_tensor("xg_dram", [CAP, D], F32).ap()
    xg_red = nc.dram_tensor("xg_red", [CAP, D], F32).ap()
    ymine = nc.dram_tensor("ymine", [HALFCAP, D], F32).ap()
    w1bf_dram = nc.dram_tensor("w1bf_dram", [128, 16 * 64 * 128], BF16).ap()
    w2bf_dram = nc.dram_tensor("w2bf_dram", [128, 4 * 64 * 512], BF16).ap()
    y_full = nc.dram_tensor("y_full", [CAP, D], F32).ap()

    p = Prog()
    p.sem("PE", "DV", "AC", "PL", "C0", "C1", "C2",
          "LC", "XT0", "XT1", "WS0", "WS1", "LT", "WR", "X00", "X01",
          "W10", "W11", "W20", "W21", "B2", "WO0", "WO1", "RL0", "RL1",
          "XS0", "XS1", "LG", "LA", "LZ", "SC0", "SC1", "XG", "YS0", "YS1",
          "XB0", "XB1", "XB2", "XB3", "XB4", "XB5", "YB0", "YB1", "YB2",
          "YB3", "OU0", "OU1", "OU2",
          "PC0", "PC1", "PB0", "PB1")

    import contextlib
    stack = contextlib.ExitStack()

    AR_BYTES = 198 * 1024
    arena = stack.enter_context(nc.sbuf_tensor("arena", [128, AR_BYTES], U8))

    def carve(off_kb, shape, dt):
        nb = mybir.dt.size(dt)
        off = int(off_kb * 1024)
        assert off % nb == 0, (off_kb, dt)
        n = 1
        for s in shape[1:]:
            n *= s
        assert off + n * nb <= AR_BYTES, (off_kb, shape, dt)
        ap = arena[0:shape[0], off:off + n * nb].bitcast(dt)
        if len(shape) == 3:
            ap = ap.rearrange("p (a b) -> p a b", b=shape[2])
        return ap

    xnT_own = [carve(4 * d, [128, T], BF16) for d in range(16)]
    cosb = carve(0, [128, TK], BF16)
    sinb = carve(4.5, [128, TK], BF16)
    rs1 = carve(9, [128, T], BF16)
    rs2 = carve(13, [128, T], BF16)
    probs = [carve(17 + 0.75 * i, [128, 384], BF16) for i in range(8)]
    probsT = [carve(189.5 + 0.75 * i, [128, 384], BF16) for i in range(2)]
    kbias = carve(23, [1, TK], BF16)
    selq = carve(27.5, [16, D], BF16)
    wobf = [carve(31.5 + 16 * i, [128, 16, 512], BF16) for i in range(2)]
    wr_bcast = carve(0, [128, D], F32)
    x0st = [carve(8 + 8 * i, [128, D], F32) for i in range(2)] + [
        carve(88 + 8 * i, [128, D], F32) for i in range(2)]
    wosc2 = [carve(24 + 8 * i, [128, D], F32) for i in range(2)] + [
        carve(104 + 8 * i, [128, D], F32) for i in range(2)]
    x1bst = [carve(40 + 4 * i, [128, D], BF16) for i in range(2)] + [
        carve(72 + 4 * i, [128, D], BF16) for i in range(2)]
    tmpa = [carve(48 + 8 * i, [128, D], F32) for i in range(2)]
    xn2scr = [carve(0 + 8 * i, [128, D], F32) for i in range(2)] + [
        carve(128 + 8 * i, [128, D], F32) for i in range(2)]
    ybuf = [carve(16 + 8 * i, [128, D], F32) for i in range(2)] + [
        carve(144 + 8 * i, [128, D], F32) for i in range(2)]
    x1bt = [carve(32 + 4 * i, [128, D], BF16) for i in range(3)] + [
        carve(160, [128, D], BF16), carve(164, [128, D], BF16),
        carve(184, [128, D], BF16)]
    yout = [carve(168 + 8 * i, [128, D], F32) for i in range(2)]
    outst = [carve(44 + 8 * i, [128, D], F32) for i in range(2)]
    xgT = [carve(0 + 0.5 * d, [128, HALFCAP], BF16) for d in range(16)]
    xgtm = [carve(16 + 8 * i, [128, D], F32) for i in range(2)]
    h_bf = [carve(32 + 0.5 * f, [128, HALFCAP], BF16) for f in range(64)]
    qfm = [carve(64 + 4 * hh, [128, T], BF16) for hh in range(16)]
    junk = carve(64, [128, D], F32)
    xnst = [carve(72 + 4 * i, [128, D], BF16) for i in range(2)]
    w1st = [carve(88 + 4 * i, [128, 2048], BF16) for i in range(2)]
    w2st = [carve(96 + 4 * i, [128, 2048], BF16) for i in range(2)]
    y_sb = [carve(74 + 2 * i, [128, 512], F32) for i in range(2)]
    junk2 = carve(80, [128, D], F32)
    wos1 = [carve(128 + 2 * i, [128, 512], F32) for i in range(2)]
    wr_rows = carve(132, [1, D], F32)
    b2rows = carve(140, [1, D], F32)
    b2row_bf = carve(148, [1, D], BF16)
    kfm = [carve(128 + 4.5 * o, [128, TK], BF16) for o in range(4)]
    vtm = [carve(146.5 + 1 * t, [128, 512], BF16) for t in range(18)]
    xnT_halo = [carve(164.5 + 0.5 * d, [128, HALO], BF16) for d in range(16)]
    xt = [carve(172.5 + 8 * i, [128, D], F32) for i in range(2)]
    wst_f = [carve(172.5 + 2 * i, [128, 512], F32) for i in range(2)]
    wst_bf = [carve(176.5 + 1 * i, [128, 512], BF16) for i in range(2)]
    ssk = carve(178.5, [4, TK], F32)
    q2st = [carve(187.5 + 1 * i, [128, 512], BF16) for i in range(2)]
    vstage = [carve(189.5 + 1 * i, [128, 512], BF16) for i in range(2)]
    invk_bf = carve(187.5, [4, TK], BF16)
    ssall = carve(164.5, [16, D], F32)
    pcst = [carve(164.5 + 8 * i, [128, D], F32) for i in range(2)]
    pcbf = [carve(180.5 + 4 * i, [128, D], BF16) for i in range(2)]
    invqT = carve(192, [128, 256], F32)
    zerot = carve(196, [128, 512], F32)

    def small(name, shape, dt):
        return stack.enter_context(nc.sbuf_tensor("s_" + name, shape, dt))

    identB = small("identB", [128, 128], BF16)
    identF = small("identF", [128, 128], F32)
    maskAB = small("maskAB", [128, 384], BF16)
    ones_bf = small("ones_bf", [1, 128], BF16)
    ones_f = small("ones_f", [1, 128], F32)
    onescol = small("onescol", [128, 1], F32)
    bigoh = small("bigoh", [128, 127], BF16)
    lmask = small("lmask", [128, 128], F32)
    myrank = small("myrank", [128, 1], F32)
    slotrow = small("slotrow", [128, 2], I32)
    b1rs = small("b1rs", [128, 64], F32)
    brs = small("brs", [1, 1], F32)
    qk16s = small("qk16s", [16, 1], F32)
    logits = small("logits", [128, 16], F32)
    Lall = small("Lall", [128, 32], F32)
    nm2s = small("nm2s", [128, 16], F32)
    r2s = small("r2s", [128, 16], F32)
    mown = small("mown", [128, 16], F32)
    umown = small("umown", [128, 16], F32)
    mfull = small("mfull", [128, 32], F32)
    scanf = small("scanf", [128, 16], F32)
    zer16 = small("zer16", [128, 16], F32)
    slotF = small("slotF", [128, 16], F32)
    slotI = small("slotI", [128, 16], I32)
    cntg = small("cntg", [128, 32], F32)
    cnt = small("cnt", [128, 1], F32)
    totp = small("totp", [128, 1], F32)
    tot0 = small("tot0", [128, 1], F32)
    offs = small("offs", [128, 1], F32)
    thrb = small("thrb", [128, 1], F32)
    lo = small("lo", [1, 1], F32)
    hi = small("hi", [1, 1], F32)
    mid = small("mid", [1, 1], F32)
    dds = small("dds", [1, 1], F32)
    ees = small("ees", [1, 1], F32)
    flag = small("flag", [1, 1], F32)
    colr = small("colr", [128, 1], F32)
    rowr = small("rowr", [1, 128], F32)
    gmax = small("gmax", [1, 1], F32)
    sigbr = small("sigbr", [128, 2], F32)
    sb12 = small("sb12", [1, 2], F32)
    rowsum = [small(f"rowsum{i}", [128, 1], F32) for i in range(8)]
    recip = [small(f"recip{i}", [128, 1], F32) for i in range(8)]
    shs = small("shs", [128, 16], F32)
    qk_bc = small("qk_bc", [128, 16], F32)
    habsn = small("habsn", [128, 16], F32)
    sgncol = small("sgncol", [16, 1], F32)
    qkrow = small("qkrow", [1, 16], F32)
    tsum = [small(f"tsum{i}", [128, 1], F32) for i in range(4)]
    tssq = [small(f"tssq{i}", [128, 1], F32) for i in range(4)]
    tssq2 = [small(f"tssq2{i}", [128, 1], F32) for i in range(4)]
    trstd = [small(f"trstd{i}", [128, 1], F32) for i in range(4)]
    tbias = [small(f"tbias{i}", [128, 1], F32) for i in range(2)]
    gmcol = [small(f"gmcol{i}", [128, 1], F32) for i in range(3)]
    tsum3 = [small(f"tsum3{i}", [128, 1], F32) for i in range(4)]
    tssq3 = [small(f"tssq3{i}", [128, 1], F32) for i in range(4)]
    trstd3 = [small(f"trstd3{i}", [128, 1], F32) for i in range(4)]
    tbias3 = [small(f"tbias3{i}", [128, 1], F32) for i in range(4)]
    tbias3b = [small(f"tbias3b{i}", [128, 1], F32) for i in range(2)]
    svec = [small(f"svec{i}", [128, 1], F32) for i in range(4)]

    pb = [stack.enter_context(nc.psum_tensor(f"pb{i}", [128, 512], F32))
          for i in range(8)]

    def pbank_bf(i):
        return pb[i][:, :].bitcast(BF16)

    A = p.add

    for dst, src in [(identB, identB_in), (identF, identF_in),
                     (maskAB, maskAB_in), (ones_bf, ones_bf_in), (ones_f, ones_f_in),
                     (onescol, onescol_in), (bigoh, bigoh_in), (lmask, lmask_in),
                     (myrank, myrank_in), (slotrow, slotrow_in), (b1rs, b1r),
                     (brs, br_in), (qk16s, qk16)]:
        A("sy", (lambda d=dst, s=src: lambda e: e.dma_start(out=d[:, :], in_=s))(),
          inc=("LC", 16))
    LC_ALL = p.v("LC")
    A("dv", lambda e: nc.vector.memset(zer16[:, :], 0.0), inc=("DV", 1))
    A("dv", lambda e: nc.vector.memset(zerot[:, :], 0.0), inc=("DV", 1))
    DV_ZERO = p.v("DV")

    ac_apply = {}
    pe_tp = {}
    dv_evac = {}
    dv_evac_bank = {0: DV_ZERO, 1: DV_ZERO}
    for i in range(NTK):
        bi = i % 2
        w = [("AC", ac_apply.get(i - 2, 0))]
        xts = "XT%d" % bi
        A("sy", (lambda i=i, bi=bi: lambda e: e.dma_start(
            out=xt[bi][:, :], in_=x_in[i * 128:(i + 1) * 128, :]))(),
          waits=w, inc=(xts, 16))
        ldv = p.v(xts)
        A("dv", (lambda bi=bi: lambda e: nc.vector.reduce_sum(
            tsum[bi][:, :], xt[bi][:, :], AX.X, negate=True))(),
          waits=[(xts, ldv), ("LC", LC_ALL)], inc=("DV", 1))
        A("dv", (lambda bi=bi: lambda e: nc.vector.tensor_scalar(
            tsum[bi][:, :], tsum[bi][:, :], 1.0 / D, None, ALU.mult))(), inc=("DV", 1))
        dv_nm = p.v("DV")
        A("ac", (lambda bi=bi: lambda e: nc.scalar.activation(
            junk[:, :], xt[bi][:, :], AF.Square, bias=tsum[bi][:, :],
            accum_out=tssq[bi][:, :]))(),
          waits=[("DV", dv_nm), ("LC", LC_ALL)], inc=("AC", 1))
        ac_sq = p.v("AC")
        A("dv", (lambda bi=bi: lambda e: nc.vector.tensor_scalar(
            trstd[bi][:, :], tssq[bi][:, :], 1.0 / D, EPS, ALU.mult, ALU.add))(),
          waits=[("AC", ac_sq)], inc=("DV", 1))
        dv_v = p.v("DV")
        A("ac", (lambda bi=bi: lambda e: nc.scalar.activation(
            trstd[bi][:, :], trstd[bi][:, :], AF.Ln))(),
          waits=[("DV", dv_v)], inc=("AC", 1))
        A("ac", (lambda bi=bi: lambda e: nc.scalar.activation(
            trstd[bi][:, :], trstd[bi][:, :], AF.Exp, scale=-0.5))(), inc=("AC", 1))
        ac_r = p.v("AC")
        A("dv", (lambda bi=bi: lambda e: nc.vector.tensor_mul(
            tbias[bi][:, :], tsum[bi][:, :], trstd[bi][:, :]))(),
          waits=[("AC", ac_r)], inc=("DV", 1))
        dv_b = p.v("DV")
        wv = [("DV", dv_b)]
        if i >= 2:
            wv.append(("PE", pe_tp.get(i - 2, 0)))
        A("ac", (lambda bi=bi: lambda e: nc.scalar.activation(
            xnst[bi][:, :], xt[bi][:, :], AF.Identity,
            bias=tbias[bi][:, :], scale=trstd[bi][:, :]))(),
          waits=wv, inc=("AC", 1))
        ac_apply[i] = p.v("AC")
        for g in range(4):
            bk = g % 2
            for j in range(4):
                d = g * 4 + j
                A("pe", (lambda bi=bi, bk=bk, j=j, d=d: lambda e: nc.tensor.transpose(
                    pbank_bf(bk)[:, j * 128:(j + 1) * 128],
                    xnst[bi][:, d * 128:(d + 1) * 128], identB[:, :]))(),
                  waits=[("AC", ac_apply[i]), ("DV", dv_evac_bank[bk]),
                         ("LC", LC_ALL)],
                  inc=("PE", 1) if j == 3 else None)
            pe_tp[i] = p.v("PE")
            d0 = g * 4
            if i < 2:
                dsts = [xnT_halo[d0 + j][:, i * 128:(i + 1) * 128] for j in range(4)]
            else:
                dsts = [xnT_own[d0 + j][:, (i - 2) * 128:(i - 1) * 128] for j in range(4)]
            for j in range(4):
                A("dv", (lambda bk=bk, j=j, dd=dsts[j]: lambda e: nc.vector.tensor_copy(
                    dd, pbank_bf(bk)[:, j * 128:(j + 1) * 128]))(),
                  waits=[("PE", pe_tp[i])], inc=("DV", 1) if j == 3 else None)
            dv_evac_bank[bk] = p.v("DV")
        dv_evac[i] = p.v("DV")
    DV_P1 = p.v("DV")
    evac_of_tile = dv_evac

    def kv_macro(m):
        if m == 0:
            return [xnT_halo[d][:, 0:HALO] for d in range(16)], HALO, evac_of_tile[1]
        mm = m - 1
        sl = [xnT_own[d][:, mm * 512:(mm + 1) * 512] for d in range(16)]
        return sl, 512, evac_of_tile[2 + mm * 4 + 3]

    def q_macro(m):
        sl = [xnT_own[d][:, m * 512:(m + 1) * 512] for d in range(16)]
        return sl, 512, evac_of_tile[2 + m * 4 + 3]

    pe_after_stage = {}
    kv_cols = [(0, HALO), (HALO, 512), (HALO + 512, 512),
               (HALO + 1024, 512), (HALO + 1536, 512)]
    dv_bankfree = DV_P1
    pe_vtp = {}
    dv_vtm = {}
    sidx = 0
    for o in range(8):
        for q in range(4):
            sbi = sidx % 2
            ws = "WS%d" % sbi
            wv = [("DV", pe_after_stage.get(sidx - 2, (0, 0))[1]),
                  ("PE", pe_after_stage.get(sidx - 2, (0, 0))[0])]
            if sidx < 2:
                wv += [("AC", p.v("AC")), ("DV", DV_P1)]
            A("sy", (lambda o=o, q=q, sbi=sbi: lambda e: e.dma_start(
                out=wst_f[sbi][:, :].rearrange("p (c m) -> p c m", m=128),
                in_=wkvT[q * 512:(q + 1) * 512, o * 128:(o + 1) * 128].rearrange(
                    "(c p) m -> p c m", p=128)))(),
              waits=wv, inc=(ws, 16))
            lwv = p.v(ws)
            A("dv", (lambda sbi=sbi: lambda e: nc.vector.tensor_copy(
                wst_bf[sbi][:, :], wst_f[sbi][:, :]))(),
              waits=[(ws, lwv), ("LC", LC_ALL)], inc=("DV", 1))
            plv = p.v("DV")
            for c in range(4):
                d = q * 4 + c
                for m in range(5):
                    slices, width, rdy = kv_macro(m)
                    first = (d == 0)
                    last = (d == 15)
                    wv = [("DV", plv), ("DV", rdy)]
                    if first:
                        wv.append(("DV", dv_bankfree))
                        wv.append(("AC", p.v("AC")))
                    A("pe", (lambda m=m, c=c, sbi=sbi, sl=slices[d], width=width,
                             first=first, last=last: lambda e: nc.tensor.matmul(
                                 pb[m][:, 0:width],
                                 wst_bf[sbi][:, c * 128:(c + 1) * 128],
                                 sl, start=first, stop=last))(),
                      waits=wv, inc=("PE", 1) if (c == 3 and m == 4) else None)
            pe_after_stage[sidx] = (p.v("PE"), plv)
            sidx += 1
        pe_done_o = p.v("PE")
        if o < 4:
            for m in range(5):
                col0, width = kv_cols[m]
                A("dv", (lambda o=o, m=m, col0=col0, width=width:
                         lambda e: nc.vector.tensor_copy(
                             kfm[o][:, col0:col0 + width], pb[m][:, 0:width]))(),
                  waits=[("PE", pe_done_o)], inc=("DV", 1))
            dv_bankfree = p.v("DV")
        else:
            ov = o - 4
            for m in range(5):
                col0, width = kv_cols[m]
                sbv = m % 2
                bk = 5 + (m % 2)
                A("dv", (lambda m=m, sbv=sbv, width=width: lambda e: nc.vector.tensor_copy(
                    vstage[sbv][:, 0:width], pb[m][:, 0:width]))(),
                  waits=[("PE", pe_done_o), ("PE", pe_vtp.get(m - 2, 0))],
                  inc=("DV", 1))
                dvv = p.v("DV")
                nblk = width // 128
                t0 = col0 // 128
                for c in range(nblk):
                    A("pe", (lambda sbv=sbv, c=c, bk=bk: lambda e: nc.tensor.transpose(
                        pbank_bf(bk)[:, c * 128:(c + 1) * 128],
                        vstage[sbv][:, c * 128:(c + 1) * 128], identB[:, :]))(),
                      waits=[("DV", dvv),
                             ("DV", dv_vtm.get(m - 2, DV_P1 if ov == 0 else 0))],
                      inc=("PE", 1) if c == nblk - 1 else None)
                pe_vtp[m] = p.v("PE")
                for c in range(nblk):
                    A("dv", (lambda c=c, t0=t0, ov=ov, bk=bk: lambda e: nc.vector.tensor_copy(
                        vtm[t0 + c][:, ov * 128:(ov + 1) * 128],
                        pbank_bf(bk)[:, c * 128:(c + 1) * 128]))(),
                      waits=[("PE", pe_vtp[m])],
                      inc=("DV", 1) if c == nblk - 1 else None)
                dv_vtm[m] = p.v("DV")
            dv_bankfree = p.v("DV")
    PE_P2 = p.v("PE")
    DV_P2 = p.v("DV")
    PL_P2 = p.v("PL")

    qss_bank_of_m = [4, 5, 6, 7]
    pe_ssq = {}
    dv_qbankfree = DV_P2
    for hh in range(16):
        for q in range(4):
            sbi = sidx % 2
            ws = "WS%d" % sbi
            A("sy", (lambda hh=hh, q=q, sbi=sbi: lambda e: e.dma_start(
                out=wst_f[sbi][:, :].rearrange("p (c m) -> p c m", m=128),
                in_=wqT[q * 512:(q + 1) * 512, hh * 128:(hh + 1) * 128].rearrange(
                    "(c p) m -> p c m", p=128)))(),
              waits=[("DV", pe_after_stage.get(sidx - 2, (0, 0))[1]),
                     ("PE", pe_after_stage.get(sidx - 2, (0, 0))[0])],
              inc=(ws, 16))
            lwv = p.v(ws)
            A("dv", (lambda sbi=sbi: lambda e: nc.vector.tensor_copy(
                wst_bf[sbi][:, :], wst_f[sbi][:, :]))(),
              waits=[(ws, lwv)], inc=("DV", 1))
            plv = p.v("DV")
            for c in range(4):
                d = q * 4 + c
                for m in range(4):
                    slices, width, rdy = q_macro(m)
                    first = (d == 0)
                    last = (d == 15)
                    wv = [("DV", plv), ("DV", rdy)]
                    if first:
                        wv.append(("DV", dv_qbankfree))
                        wv.append(("AC", p.v("AC")))
                    A("pe", (lambda m=m, c=c, sbi=sbi, sl=slices[d], first=first,
                             last=last: lambda e: nc.tensor.matmul(
                                 pb[m][:, :], wst_bf[sbi][:, c * 128:(c + 1) * 128],
                                 sl, start=first, stop=last))(),
                      waits=wv, inc=("PE", 1) if (c == 3 and m == 3) else None)
            pe_after_stage[sidx] = (p.v("PE"), plv)
            sidx += 1
        pe_done_h = p.v("PE")
        for m in range(4):
            wv = [("PE", pe_done_h), ("PE", pe_ssq.get(hh * 4 + m - 2, 0))]
            if hh == 0:
                wv.append(("DV", DV_P2))
            A("ac", (lambda m=m: lambda e: nc.scalar.activation(
                q2st[m % 2][:, :], pb[m][:, :], AF.Square))(),
              waits=wv, inc=("AC", 1))
            acsq = p.v("AC")
            wv = [("AC", acsq)]
            if hh == 0:
                wv.append(("DV", DV_P2))
            A("pe", (lambda m=m, hh=hh: lambda e: nc.tensor.matmul(
                pb[qss_bank_of_m[m]][0:32, :], bigoh[:, 63 - hh:95 - hh],
                q2st[m % 2][:, :], start=(hh == 0), stop=(hh == 15)))(),
              waits=wv, inc=("PE", 1))
            pe_ssq[hh * 4 + m] = p.v("PE")
            A("dv", (lambda hh=hh, m=m: lambda e: nc.vector.tensor_copy(
                qfm[hh][:, m * 512:(m + 1) * 512], pb[m][:, :]))(),
              waits=[("AC", acsq)], inc=("DV", 1))
        dv_qbankfree = p.v("DV")
    PE_P3 = p.v("PE")
    PL_P3 = p.v("PL")
    for m in range(4):
        A("dv", (lambda m=m: lambda e: nc.vector.tensor_copy(
            ssall[0:16, m * 512:(m + 1) * 512], pb[qss_bank_of_m[m]][0:16, :]))(),
          waits=[("PE", PE_P3)], inc=("DV", 1))
    DV_SSQ = p.v("DV")

    for dst, src in [(cosb, cosb_in), (sinb, sinb_in), (kbias, kbias_in),
                     (selq, selq_in)]:
        A("sy", (lambda d=dst, s=src: lambda e: e.dma_start(out=d[:, :], in_=s))(),
          waits=[("PE", PE_P3)], inc=("LT", 16))
    LW_TRIG = p.v("LT")

    dv_kss = DV_SSQ
    pe_kssm = {}
    for m in range(5):
        col0, width = kv_cols[m]
        for o in range(4):
            j = m * 4 + o
            A("ac", (lambda o=o, col0=col0, width=width: lambda e: nc.scalar.activation(
                q2st[o % 2][:, 0:width], kfm[o][:, col0:col0 + width], AF.Square))(),
              waits=[("DV", DV_P2), ("PE", PE_P3), ("PE", pe_kssm.get(j - 2, 0))],
              inc=("AC", 1))
            acsq = p.v("AC")
            A("pe", (lambda o=o, width=width, m=m: lambda e: nc.tensor.matmul(
                pb[7][0:32, 0:width], bigoh[:, 63 - o:95 - o],
                q2st[o % 2][:, 0:width], start=(o == 0), stop=(o == 3)))(),
              waits=[("AC", acsq), ("DV", dv_kss if o == 0 else 0),
                     ("DV", DV_SSQ if (m == 0 and o == 0) else 0)],
              inc=("PE", 1))
            pe_kssm[j] = p.v("PE")
        pe_kss = p.v("PE")
        A("dv", (lambda m=m, col0=col0, width=width: lambda e: nc.vector.tensor_copy(
            ssk[0:4, col0:col0 + width], pb[7][0:4, 0:width]))(),
          waits=[("PE", pe_kss)], inc=("DV", 1))
        dv_kss = p.v("DV")
    A("dv", lambda e: nc.vector.tensor_scalar_max(ssk[:, :], ssk[:, :], 1e-24),
      waits=[("DV", dv_kss)], inc=("DV", 1))
    dvk1 = p.v("DV")
    A("ac", lambda e: nc.scalar.activation(ssk[:, :], ssk[:, :], AF.Ln),
      waits=[("DV", dvk1)], inc=("AC", 1))
    A("ac", lambda e: nc.scalar.activation(ssk[:, :], ssk[:, :], AF.Exp, scale=-0.5),
      inc=("AC", 1))
    ack = p.v("AC")
    A("dv", lambda e: nc.vector.tensor_copy(invk_bf[:, :], ssk[:, :]),
      waits=[("AC", ack)], inc=("DV", 1))
    DV_INVK = p.v("DV")
    A("dv", lambda e: nc.vector.tensor_scalar_max(ssall[:, :], ssall[:, :], 1e-24),
      waits=[("DV", DV_SSQ)], inc=("DV", 1))
    dvq1 = p.v("DV")
    A("ac", lambda e: nc.scalar.activation(ssall[:, :], ssall[:, :], AF.Ln),
      waits=[("DV", dvq1)], inc=("AC", 1))
    A("ac", lambda e: nc.scalar.activation(ssall[:, :], ssall[:, :], AF.Exp, scale=-0.5),
      inc=("AC", 1))
    acq = p.v("AC")
    A("dv", lambda e: nc.vector.tensor_scalar(
        ssall[:, :], ssall[:, :], qk16s[:, :], None, ALU.mult),
      waits=[("AC", acq)], inc=("DV", 1))
    dvq2 = p.v("DV")
    A("ac", lambda e: nc.scalar.activation(ssall[:, :], ssall[:, :], AF.Abs),
      waits=[("DV", dvq2)], inc=("AC", 1))
    ACQ_ABS = p.v("AC")
    dv_iqt = {6: dv_kss, 7: dv_kss}
    for t in range(16):
        bk = 6 + (t % 2)
        A("pe", (lambda t=t, bk=bk: lambda e: nc.tensor.transpose(
            pb[bk][:, 0:16], ssall[0:16, t * 128:(t + 1) * 128],
            identF[0:16, 0:16]))(),
          waits=[("AC", ACQ_ABS), ("DV", dv_iqt[bk]), ("PE", pe_kss)],
          inc=("PE", 1))
        pev = p.v("PE")
        A("dv", (lambda t=t, bk=bk: lambda e: nc.vector.tensor_copy(
            invqT[:, t * 16:(t + 1) * 16], pb[bk][:, 0:16]))(),
          waits=[("PE", pev)], inc=("DV", 1))
        dv_iqt[bk] = p.v("DV")
    A("pe", lambda e: nc.tensor.transpose(pb[6][0:1, 0:16], qk16s[:, :],
                                          identF[0:16, 0:16]),
      waits=[("LC", LC_ALL), ("DV", dv_iqt[6])], inc=("PE", 1))
    pev = p.v("PE")
    A("dv", lambda e: nc.vector.tensor_copy(qkrow[:, :], pb[6][0:1, 0:16]),
      waits=[("PE", pev)], inc=("DV", 1))
    dvr = p.v("DV")
    A("pe", lambda e: nc.tensor.matmul(pb[6][:, 0:16], ones_f[:, :], qkrow[:, :],
                                       start=True, stop=True),
      waits=[("DV", dvr)], inc=("PE", 1))
    pev = p.v("PE")
    A("dv", lambda e: nc.vector.tensor_copy(qk_bc[:, :], pb[6][:, 0:16]),
      waits=[("PE", pev)], inc=("DV", 1))
    dvq = p.v("DV")
    A("ac", lambda e: nc.scalar.activation(shs[:, :], qk_bc[:, :], AF.Sign),
      waits=[("DV", dvq)], inc=("AC", 1))
    A("ac", lambda e: nc.scalar.activation(habsn[:, :], qk_bc[:, :], AF.Abs),
      inc=("AC", 1))
    achb = p.v("AC")
    A("dv", lambda e: nc.vector.tensor_scalar(habsn[:, :], habsn[:, :], -1.0, None,
                                              ALU.mult),
      waits=[("AC", achb)], inc=("DV", 1))
    A("dv", lambda e: nc.vector.tensor_copy(shs[:, :], shs[:, :]),
      inc=("DV", 1))
    DV_PREP = p.v("DV")

    def rope_k(fm, o):
        for idx, (c0, width) in enumerate(kv_cols):
            bk = 6 + (idx % 2)
            A("pe", (lambda bk=bk, o=o, c0=c0, width=width:
                     lambda e: nc.tensor.matmul(
                         pb[bk][:, 0:width], selq[0:4, o * 128:(o + 1) * 128],
                         invk_bf[0:4, c0:c0 + width], start=True, stop=True))(),
              waits=[("DV", DV_INVK), ("LT", LW_TRIG), ("DV", p.v("DV")),
                     ("DV", DV_PREP)],
              inc=("PE", 1))
            pev = p.v("PE")
            A("dv", (lambda fm=fm, c0=c0, width=width, bk=bk:
                     lambda e: nc.vector.tensor_mul(
                         fm[:, c0:c0 + width], fm[:, c0:c0 + width],
                         pb[bk][:, 0:width]))(),
              waits=[("PE", pev)], inc=("DV", 1))
            A("dv", (lambda fm=fm, c0=c0, width=width:
                     lambda e: nc.vector.tensor_mul(
                         rs1[:, 0:width], fm[:, c0:c0 + width],
                         cosb[:, c0:c0 + width]))(), inc=("DV", 1))
            A("dv", (lambda fm=fm, c0=c0, width=width:
                     lambda e: nc.vector.tensor_mul(
                         rs2[0:64, 0:width], fm[64:128, c0:c0 + width],
                         sinb[64:128, c0:c0 + width]))(), inc=("DV", 1))
            A("dv", (lambda fm=fm, c0=c0, width=width:
                     lambda e: nc.vector.tensor_mul(
                         rs2[64:128, 0:width], fm[0:64, c0:c0 + width],
                         sinb[0:64, c0:c0 + width]))(), inc=("DV", 1))
            A("dv", (lambda fm=fm, c0=c0, width=width: lambda e: nc.vector.tensor_add(
                fm[:, c0:c0 + width], rs1[:, 0:width], rs2[:, 0:width]))(),
              inc=("DV", 1))
        return p.v("DV")

    dv_krope = {}
    for o in range(4):
        dv_krope[o] = rope_k(kfm[o], o)

    qcols = [(0, 2048)]
    ac_exp = {}
    pe_ptp = {}
    dv_ptevac = {}
    dv_avevac = {}
    width = 2048

    def rope_q_t1(hh):
        A("dv", (lambda hh=hh: lambda e: nc.vector.tensor_mul(
            rs1[:, 0:width], qfm[hh][:, 0:width],
            cosb[:, HALO:HALO + width]))(),
          waits=[("DV", DV_PREP), ("LT", LW_TRIG)], inc=("DV", 1))
        return p.v("DV")

    def rope_q_pla(hh, dvt1):
        A("pl", (lambda hh=hh: lambda e: nc.gpsimd.tensor_mul(
            rs2[0:64, 0:width], qfm[hh][64:128, 0:width],
            sinb[64:128, HALO:HALO + width]))(),
          waits=[("DV", DV_PREP), ("LT", LW_TRIG), ("DV", dvt1)],
          inc=("PL", 1))

    def rope_q_plb(hh):
        A("pl", (lambda hh=hh: lambda e: nc.gpsimd.tensor_mul(
            rs2[64:128, 0:width], qfm[hh][0:64, 0:width],
            sinb[0:64, HALO:HALO + width]))(),
          inc=("PL", 1))
        return p.v("PL")

    def rope_q_add(hh, plr):
        A("dv", (lambda hh=hh: lambda e: nc.vector.tensor_add(
            rs1[:, 0:width], rs1[:, 0:width], rs2[:, 0:width]))(),
          waits=[("PL", plr)], inc=("DV", 1))

    def rope_q_fin(hh):
        A("dv", (lambda hh=hh: lambda e: nc.vector.tensor_scalar(
            qfm[hh][:, 0:width], rs1[:, 0:width], shs[:, hh:hh + 1],
            None, ALU.mult))(), inc=("DV", 1))
        return p.v("DV")

    st5 = {}

    def p5_front(it, hh, t, kv, dv_qr):
        sb3 = it % 3
        sb6 = it % 8
        cb = 3 + it % 4
        wv = [("DV", dv_qr[0]), ("PL", dv_qr[1]), ("DV", dv_krope[kv]),
              ("AC", ac_exp.get(it - 3, 0)), ("LT", LW_TRIG)]
        if it < 3:
            wv.append(("DV", DV_SSQ))
        A("pe", (lambda hh=hh, t=t, sb3=sb3, kv=kv: lambda e: nc.tensor.matmul(
            pb[sb3][:, 0:384], qfm[hh][:, t * 128:(t + 1) * 128],
            kfm[kv][:, t * 128:t * 128 + 384], start=True, stop=False))(),
          waits=wv)
        lastm = (t >= 2)
        A("pe", (lambda sb3=sb3, lastm=lastm: lambda e: nc.tensor.matmul(
            pb[sb3][:, 0:384], identB[:, :], maskAB[:, :],
            start=False, stop=lastm))(),
          inc=("PE", 1) if lastm else None)
        if not lastm:
            A("pe", (lambda sb3=sb3, t=t: lambda e: nc.tensor.matmul(
                pb[sb3][:, 0:384], ones_bf[:, :],
                kbias[0:1, t * 128:t * 128 + 384], start=False, stop=True))(),
              inc=("PE", 1))
        pe_sc = p.v("PE")
        A("ac", (lambda sb3=sb3, sb6=sb6, hh=hh, t=t: lambda e: nc.scalar.activation(
            probs[sb6][:, :], pb[sb3][:, 0:384], AF.Exp,
            bias=habsn[:, hh:hh + 1],
            scale=invqT[:, t * 16 + hh:t * 16 + hh + 1],
            accum_out=rowsum[sb6][:, :]))(),
          waits=[("PE", pe_sc), ("PE", pe_ptp.get(it - 8, 0)), ("DV", DV_PREP)],
          inc=("AC", 1))
        ac_exp[it] = p.v("AC")
        A("dv", (lambda sb6=sb6: lambda e: nc.vector.reciprocal(
            recip[sb6][:, :], rowsum[sb6][:, :]))(),
          waits=[("AC", ac_exp[it])], inc=("DV", 1))
        A("dv", (lambda sb6=sb6: lambda e: nc.vector.tensor_scalar(
            probs[sb6][:, :], probs[sb6][:, :], recip[sb6][:, :], None,
            ALU.mult))(),
          inc=("DV", 1))
        st5[it] = (hh, t, kv, sb3, sb6, cb, p.v("DV"))

    st5b = {}

    def p5_back1(it):
        hh, t, kv, sb3, sb6, cb, dv_psc = st5.pop(it)
        sb2 = it % 2
        for c in range(3):
            wv = [("DV", dv_psc), ("DV", dv_ptevac.get(it - 4, 0))]
            if it < 4:
                wv.append(("DV", dv_qbankfree))
            A("pe", (lambda cb=cb, sb6=sb6, c=c: lambda e: nc.tensor.transpose(
                pbank_bf(cb)[:, c * 128:(c + 1) * 128],
                probs[sb6][:, c * 128:(c + 1) * 128], identB[:, :]))(),
              waits=wv, inc=("PE", 1) if c == 2 else None)
        pe_ptp[it] = p.v("PE")
        wv = [("PE", pe_ptp[it])]
        if it < 2:
            wv.append(("DV", dv_krope[3]))
        A("dv", (lambda cb=cb, sb2=sb2: lambda e: nc.vector.tensor_copy(
            probsT[sb2][:, :], pbank_bf(cb)[:, 0:384]))(),
          waits=wv, inc=("DV", 1))
        dv_ptevac[it] = p.v("DV")
        st5b[it] = (hh, t, kv, cb, sb2)

    def p5_back2(it):
        hh, t, kv, cb, sb2 = st5b.pop(it)
        for c in range(3):
            wv = [("DV", dv_ptevac[it]), ("DV", DV_P2)]
            if (it - 4) in dv_avevac:
                wv.append(dv_avevac[it - 4])
            A("pe", (lambda cb=cb, sb2=sb2, c=c, t=t, kv=kv:
                     lambda e: nc.tensor.matmul(
                         pb[cb][:, 256:384],
                         vtm[t + c][:, kv * 128:(kv + 1) * 128],
                         probsT[sb2][:, c * 128:(c + 1) * 128],
                         start=(c == 0), stop=(c == 2)))(),
                  waits=wv, inc=("PE", 1) if c == 2 else None)
        pe_av = p.v("PE")
        A("ac", (lambda cb=cb, hh=hh, t=t: lambda e: nc.scalar.copy(
            qfm[hh][:, t * 128:(t + 1) * 128], pb[cb][:, 256:384]))(),
          waits=[("PE", pe_av)], inc=("AC", 1))
        dv_avevac[it] = ("AC", p.v("AC"))

    pc_state = {"j": 0, "hist": {0: [], 1: []}}

    def pc_chunk():
        j = pc_state["j"]
        if j >= 128:
            return
        pc_state["j"] = j + 1
        bi = j % 2
        srcap = w1h if j < 64 else w2
        dstap = w1bf_dram if j < 64 else w2bf_dram
        col = (j % 64) * 2048
        sts = "PB%d" % bi
        hist = pc_state["hist"][bi]
        wv = [(sts, hist[-16])] if len(hist) >= 16 else []
        A("gp", (lambda srcap=srcap, dstap=dstap, col=col: lambda e:
                 e.dma_start(out=dstap[:, col:col + 2048],
                             in_=srcap[:, col:col + 2048]))(),
          waits=wv, inc=(sts, 16))
        hist.append(p.v(sts))

    _t1 = rope_q_t1(0)
    rope_q_pla(0, _t1)
    _plr = rope_q_plb(0)
    rope_q_add(0, _plr)
    rope_next = (rope_q_fin(0), _plr)
    for hh in range(16):
        dv_qr = rope_next
        kv = hh // 4
        for t in range(16):
            it = hh * 16 + t
            p5_front(it, hh, t, kv, dv_qr)
            if hh < 15:
                if t == 2:
                    _t1 = rope_q_t1(hh + 1)
                elif t == 3:
                    rope_q_pla(hh + 1, _t1)
                elif t == 4:
                    _plr = rope_q_plb(hh + 1)
                elif t == 7:
                    rope_q_add(hh + 1, _plr)
                elif t == 8:
                    rope_next = (rope_q_fin(hh + 1), _plr)
            if t % 2 == 0:
                pc_chunk()
            if it >= 4:
                p5_back1(it - 4)
            if it >= 5:
                p5_back2(it - 5)
    for j in range(252, 256):
        p5_back2(j - 1)
        p5_back1(j)
    p5_back2(255)
    PE_P5 = p.v("PE")
    DV_P5 = p.v("DV")
    AC_P5 = p.v("AC")

    pl_wocast = {}
    wo_last = {0: None, 1: None}
    wo_pb_last = {0: 0, 1: 0}
    pe_wo = {}
    dv_woevac = {}
    gd_wodma = {}
    for dm in range(4):
        for ch in range(16):
            sbi = (dm * 16 + ch) % 2
            wv = [("AC", pl_wocast.get(dm * 16 + ch - 2, 0))]
            if dm == 0 and ch < 2:
                wv += [("PE", PE_P5), ("DV", DV_P5), ("AC", AC_P5)]
            ws = "WS%d" % sbi
            A("sy", (lambda dm=dm, ch=ch, sbi=sbi: lambda e: e.dma_start(
                out=wst_f[sbi][:, :],
                in_=woT[ch * 128:(ch + 1) * 128, dm * 512:(dm + 1) * 512]))(),
              waits=wv, inc=(ws, 16))
            lwv = p.v(ws)
            wv = [(ws, lwv)]
            if dm < 2 and ch == 0:
                wv += [("PE", PE_P5), ("DV", DV_P5), ("AC", AC_P5)]
            else:
                wv.append(("PE", pe_wo.get((dm - 2, 15), 0)))
            A("ac", (lambda dm=dm, ch=ch, sbi=sbi: lambda e: nc.scalar.copy(
                wobf[dm % 2][:, ch, :], wst_f[sbi][:, :]))(),
              waits=wv, inc=("AC", 1))
            pl_wocast[dm * 16 + ch] = p.v("AC")
        plw = p.v("AC")
        for t in range(16):
            for ch in range(16):
                first = (ch == 0)
                wv = [("AC", plw)]
                if first:
                    wv.append(("DV", wo_pb_last[t % 2]))
                    if dm == 0:
                        wv.append(dv_avevac[15 * 16 + t])
                        if t < 2:
                            wv.append(("AC", AC_P5))
                A("pe", (lambda dm=dm, t=t, ch=ch, first=first:
                         lambda e: nc.tensor.matmul(
                             pb[t % 2][:, :], qfm[ch][:, t * 128:(t + 1) * 128],
                             wobf[dm % 2][:, ch, :], start=first, stop=(ch == 15)))(),
                  waits=wv, inc=("PE", 1) if ch == 15 else None)
            pe_wo[(dm, t)] = p.v("PE")
            A("dv", (lambda dm=dm, t=t: lambda e: nc.vector.tensor_copy(
                wos1[t % 2][:, :], pb[t % 2][:, :]))(),
              waits=[("PE", pe_wo[(dm, t)])] + (
                  [wo_last[t % 2]] if wo_last[t % 2] else []),
              inc=("DV", 1))
            dv_woevac[(dm, t)] = p.v("DV")
            wo_pb_last[t % 2] = p.v("DV")
            wos = "WO%d" % (t % 2)
            A("gp", (lambda dm=dm, t=t: lambda e: e.dma_start(
                out=wo_dram[t * 128:(t + 1) * 128, dm * 512:(dm + 1) * 512],
                in_=wos1[t % 2][:, :]))(),
              waits=[("DV", dv_woevac[(dm, t)])], inc=(wos, 16))
            gd_wodma[(dm, t)] = (wos, p.v(wos))
            wo_last[t % 2] = (wos, p.v(wos))
    GD_WO = [("WO0", p.v("WO0")), ("WO1", p.v("WO1"))]
    PE_WOD = p.v("PE")

    A("sy", lambda e: e.dma_start(out=wr_rows[:, :], in_=wr_row),
      waits=[("PE", PE_P5), ("DV", DV_P5)], inc=("WR", 16))
    LW_WR = p.v("WR")
    A("dv", lambda e: nc.vector.reduce_sum(sb12[0:1, 0:1], wr_rows[:, :], AX.X),
      waits=[("WR", LW_WR)], inc=("DV", 1))
    A("dv", lambda e: nc.vector.tensor_copy(sb12[0:1, 1:2], brs[:, :]), inc=("DV", 1))
    dv_s12 = p.v("DV")
    for qq in range(4):
        A("pe", (lambda qq=qq: lambda e: nc.tensor.matmul(
            pb[6][:, :], ones_f[:, :], wr_rows[0:1, qq * 512:(qq + 1) * 512],
            start=True, stop=True))(),
          waits=[("PE", PE_P5), ("DV", p.v("DV"))], inc=("PE", 1))
        pev = p.v("PE")
        A("dv", (lambda qq=qq: lambda e: nc.vector.tensor_copy(
            wr_bcast[:, qq * 512:(qq + 1) * 512], pb[6][:, :]))(),
          waits=[("PE", pev)], inc=("DV", 1))
    A("pe", lambda e: nc.tensor.matmul(
        pb[6][:, 0:2], ones_f[:, :], sb12[:, :], start=True, stop=True),
      waits=[("DV", dv_s12), ("DV", p.v("DV"))], inc=("PE", 1))
    pev = p.v("PE")
    A("dv", lambda e: nc.vector.tensor_copy(sigbr[:, :], pb[6][:, 0:2]),
      waits=[("PE", pev)], inc=("DV", 1))
    DV_SIG = p.v("DV")

    gd_x1w = {}
    gd_wold = {}
    gd_x0 = {}
    dv_s1 = {}
    dv_trstd = {}
    dv_add = {}
    dv_nm2ts = {}
    dv_r2sts = {}
    ac_apply4 = {}
    ac_idacc = {}
    ac_sqln2 = {}
    ac_expr2 = {}

    def s2_loads(t):
        rl = "RL%d" % (t % 2)
        wv = list(GD_WO) + [(rl, p.v(rl))]
        if (t - 4) in ac_apply4:
            wv.append(("AC", ac_apply4[t - 4]))
        A("gp", (lambda t=t: lambda e: e.dma_start(
            out=wosc2[t % 4][:, :], in_=wo_dram[t * 128:(t + 1) * 128, :]))(),
          waits=wv, inc=(rl, 16))
        gd_wold[t] = (rl, p.v(rl))
        x0s = "X0%d" % (t % 2)
        wv = [(x0s, p.v(x0s))]
        if (t - 4) in dv_add:
            wv.append(("DV", dv_add[t - 4]))
        if t < 4:
            wv.append(("PE", PE_WOD))
        if t < 2:
            wv.append(("DV", DV_SIG))
        A("sy", (lambda t=t: lambda e: e.dma_start(
            out=x0st[t % 4][:, :],
            in_=x_in[HALO + t * 128:HALO + (t + 1) * 128, :]))(),
          waits=wv, inc=(x0s, 16))
        gd_x0[t] = (x0s, p.v(x0s))

    def s2_stats1(t):
        wv = [gd_wold[t]]
        if (t - 4) in ac_apply4:
            wv.append(("AC", ac_apply4[t - 4]))
        A("dv", (lambda t=t: lambda e: nc.vector.reduce_sum(
            tsum[t % 4][:, :], wosc2[t % 4][:, :], AX.X, negate=True))(),
          waits=wv, inc=("DV", 1))
        A("dv", (lambda t=t: lambda e: nc.vector.tensor_scalar(
            tsum[t % 4][:, :], tsum[t % 4][:, :], 1.0 / D, None,
            ALU.mult))(), inc=("DV", 1))
        dv_s1[t] = p.v("DV")

    def s2_trstd(t):
        A("dv", (lambda t=t: lambda e: nc.vector.tensor_scalar(
            trstd[t % 4][:, :], tssq[t % 4][:, :], 1.0 / D, EPS,
            ALU.mult, ALU.add))(),
          waits=[("AC", ac_sq1[t]), ("AC", ac_apply4.get(t - 4, 0))],
          inc=("DV", 1))
        dv_trstd[t] = p.v("DV")

    def s2_apply(t):
        A("ac", (lambda t=t: lambda e: nc.scalar.activation(
            trstd[t % 4][:, :], trstd[t % 4][:, :], AF.Ln))(),
          waits=[("DV", dv_trstd[t])], inc=("AC", 1))
        A("ac", (lambda t=t: lambda e: nc.scalar.activation(
            trstd[t % 4][:, :], trstd[t % 4][:, :], AF.Exp, scale=-0.5))(),
          inc=("AC", 1))
        A("ac", (lambda t=t: lambda e: nc.scalar.activation(
            tbias[t % 2][:, :], tsum[t % 4][:, :], AF.Identity,
            scale=trstd[t % 4][:, :]))(), inc=("AC", 1))
        wv = []
        if (t - 2) in dv_add:
            wv.append(("DV", dv_add[t - 2]))
        A("ac", (lambda t=t: lambda e: nc.scalar.activation(
            tmpa[t % 2][:, :], wosc2[t % 4][:, :], AF.Identity,
            bias=tbias[t % 2][:, :], scale=trstd[t % 4][:, :]))(),
          waits=wv, inc=("AC", 1))
        ac_apply4[t] = p.v("AC")

    def s2_add(t):
        wv = [("AC", ac_apply4[t]), gd_x0[t]]
        if (t - 4) in gd_x1w:
            wv.append(gd_x1w[t - 4])
        A("dv", (lambda t=t: lambda e: nc.vector.tensor_add(
            x1bst[t % 4][:, :], tmpa[t % 2][:, :], x0st[t % 4][:, :]))(),
          waits=wv, inc=("DV", 1))
        dv_add[t] = p.v("DV")

    def s2_spill(t):
        xs = "XS%d" % (t % 2)
        A("gp", (lambda t=t: lambda e: e.dma_start(
            out=x1_dram[t * 128:(t + 1) * 128, :], in_=x1bst[t % 4][:, :]))(),
          waits=[("DV", dv_add[t]), (xs, p.v(xs))], inc=(xs, 16))
        gd_x1w[t] = (xs, p.v(xs))

    def s2_gate(t):
        A("dv", (lambda t=t: lambda e: nc.vector.tensor_mul(
            junk2[:, :], x1bst[t % 4][:, :], wr_bcast[:, :]))(),
          waits=[("DV", DV_SIG)], inc=("DV", 1))
        A("dv", (lambda t=t: lambda e: nc.vector.reduce_sum(
            svec[t % 4][:, :], junk2[:, :], AX.X))(), inc=("DV", 1))

    def s2_idacc(t):
        A("ac", (lambda t=t: lambda e: nc.scalar.activation(
            junk[:, :], x1bst[t % 4][:, :], AF.Identity,
            accum_out=nm2s[:, t:t + 1]))(),
          waits=[("DV", dv_add[t])], inc=("AC", 1))
        ac_idacc[t] = p.v("AC")

    def s2_nm2ts(t):
        A("dv", (lambda t=t: lambda e: nc.vector.tensor_scalar(
            nm2s[:, t:t + 1], nm2s[:, t:t + 1], -1.0 / D, None,
            ALU.mult))(),
          waits=[("AC", ac_idacc[t])], inc=("DV", 1))
        dv_nm2ts[t] = p.v("DV")

    def s2_sqln2(t):
        A("ac", (lambda t=t: lambda e: nc.scalar.activation(
            junk[:, :], x1bst[t % 4][:, :], AF.Square, bias=nm2s[:, t:t + 1],
            accum_out=tssq2[t % 4][:, :]))(),
          waits=[("DV", dv_nm2ts[t]), ("DV", dv_r2sts.get(t - 4, 0))],
          inc=("AC", 1))
        ac_sqln2[t] = p.v("AC")

    def s2_r2sts(t):
        A("dv", (lambda t=t: lambda e: nc.vector.tensor_scalar(
            r2s[:, t:t + 1], tssq2[t % 4][:, :], 1.0 / D, EPS,
            ALU.mult, ALU.add))(),
          waits=[("AC", ac_sqln2[t])], inc=("DV", 1))
        dv_r2sts[t] = p.v("DV")

    def s2_expr2(t):
        A("ac", (lambda t=t: lambda e: nc.scalar.activation(
            r2s[:, t:t + 1], r2s[:, t:t + 1], AF.Ln))(),
          waits=[("DV", dv_r2sts[t])], inc=("AC", 1))
        A("ac", (lambda t=t: lambda e: nc.scalar.activation(
            r2s[:, t:t + 1], r2s[:, t:t + 1], AF.Exp, scale=-0.5))(),
          inc=("AC", 1))
        ac_expr2[t] = p.v("AC")

    def s2_logit(t):
        A("dv", (lambda t=t: lambda e: nc.vector.scalar_tensor_tensor(
            svec[t % 4][:, :], nm2s[:, t:t + 1], sigbr[:, 0:1],
            svec[t % 4][:, :], ALU.mult, ALU.add))(),
          waits=[("AC", ac_expr2[t])], inc=("DV", 1))
        A("dv", (lambda t=t: lambda e: nc.vector.tensor_mul(
            svec[t % 4][:, :], svec[t % 4][:, :], r2s[:, t:t + 1]))(),
          inc=("DV", 1))
        A("dv", (lambda t=t: lambda e: nc.vector.tensor_add(
            logits[:, t:t + 1], svec[t % 4][:, :], sigbr[:, 1:2]))(),
          inc=("DV", 1))

    ac_sq1 = {}

    def s2_sq1v(t):
        A("ac", (lambda t=t: lambda e: nc.scalar.activation(
            junk[:, :], wosc2[t % 4][:, :], AF.Square, bias=tsum[t % 4][:, :],
            accum_out=tssq[t % 4][:, :]))(),
          waits=[("DV", dv_s1[t]), ("DV", dv_trstd.get(t - 4, 0))],
          inc=("AC", 1))
        ac_sq1[t] = p.v("AC")

    for s in range(24):
        if 1 <= s <= 16:
            s2_stats1(s - 1)
        if 6 <= s <= 21:
            s2_nm2ts(s - 6)
        if 7 <= s <= 22:
            s2_r2sts(s - 7)
        if 5 <= s <= 20:
            s2_gate(s - 5)
        if 4 <= s <= 19:
            s2_add(s - 4)
        if s < 16:
            s2_loads(s)
        if 3 <= s <= 18:
            s2_apply(s - 3)
        if 2 <= s <= 17:
            s2_sq1v(s - 2)
        if 5 <= s <= 20:
            s2_idacc(s - 5)
        if 6 <= s <= 21:
            s2_sqln2(s - 6)
        if 7 <= s <= 22:
            s2_expr2(s - 7)
        if 2 <= s <= 17:
            s2_trstd(s - 2)
        if 8 <= s <= 23:
            s2_logit(s - 8)
        if 4 <= s <= 19:
            s2_spill(s - 4)
    DV_A3 = p.v("DV")
    GD_X1 = [("XS0", p.v("XS0")), ("XS1", p.v("XS1"))]
    AC_A3 = p.v("AC")

    A("gp", lambda e: e.dma_start(
        out=log_db.rearrange("o (pp t) -> (o pp) t", t=16), in_=logits[:, :]),
      waits=[("DV", DV_A3)], inc=("LG", 16))
    gdl = p.v("LG")
    A("gp", lambda e: nc.gpsimd.collective_compute(
        "AllGather", ALU.bypass,
        replica_groups=[[0, 1], [2, 3], [4, 5], [6, 7]],
        ins=[log_db.opt()], outs=[log_gth.opt()]),
      waits=[("LG", gdl)], inc=("C0", 1))
    for r in range(2):
        A("gp", (lambda r=r: lambda e: e.dma_start(
            out=Lall[:, r * 16:(r + 1) * 16],
            in_=log_gth[r:r + 1, :].rearrange("o (pp t) -> (o pp) t", t=16)))(),
          waits=[("C0", 1)], inc=("LA", 16))
    GD_LALL = p.v("LA")
    A("dv", lambda e: nc.vector.reduce_max(colr[:, :], Lall[:, :], AX.X),
      waits=[("LA", GD_LALL)], inc=("DV", 1))
    dv1 = p.v("DV")
    A("pe", lambda e: nc.tensor.transpose(pb[6][0:1, 0:128], colr[:, :], identF[:, :]),
      waits=[("DV", dv1), ("DV", DV_A3)], inc=("PE", 1))
    pev = p.v("PE")
    A("dv", lambda e: nc.vector.tensor_copy(rowr[:, :], pb[6][0:1, 0:128]),
      waits=[("PE", pev)], inc=("DV", 1))
    A("dv", lambda e: nc.vector.reduce_max(gmax[:, :], rowr[:, :], AX.X), inc=("DV", 1))
    A("dv", lambda e: nc.vector.tensor_scalar(hi[:, :], gmax[:, :], 1.0, None, ALU.add),
      inc=("DV", 1))
    A("dv", lambda e: nc.vector.tensor_scalar(cntg[:, :], Lall[:, :], -1.0, None,
                                              ALU.mult), inc=("DV", 1))
    A("dv", lambda e: nc.vector.reduce_max(colr[:, :], cntg[:, :], AX.X), inc=("DV", 1))
    dv1 = p.v("DV")
    A("pe", lambda e: nc.tensor.transpose(pb[6][0:1, 0:128], colr[:, :], identF[:, :]),
      waits=[("DV", dv1)], inc=("PE", 1))
    pev = p.v("PE")
    A("dv", lambda e: nc.vector.tensor_copy(rowr[:, :], pb[6][0:1, 0:128]),
      waits=[("PE", pev)], inc=("DV", 1))
    A("dv", lambda e: nc.vector.reduce_max(lo[:, :], rowr[:, :], AX.X), inc=("DV", 1))
    A("dv", lambda e: nc.vector.tensor_scalar(lo[:, :], lo[:, :], -1.0, -1.0,
                                              ALU.mult, ALU.add), inc=("DV", 1))
    for it in range(BISECT):
        A("dv", lambda e: nc.vector.tensor_add(mid[:, :], lo[:, :], hi[:, :]),
          inc=("DV", 1))
        A("dv", lambda e: nc.vector.tensor_scalar(mid[:, :], mid[:, :], 0.5, None,
                                                  ALU.mult), inc=("DV", 1))
        dvm = p.v("DV")
        A("pe", lambda e: nc.tensor.matmul(pb[6][:, 0:1], ones_f[:, :], mid[:, :],
                                           start=True, stop=True),
          waits=[("DV", dvm)], inc=("PE", 1))
        pev = p.v("PE")
        A("dv", lambda e: nc.vector.tensor_copy(thrb[:, :], pb[6][:, 0:1]),
          waits=[("PE", pev)], inc=("DV", 1))
        A("dv", lambda e: nc.vector.tensor_scalar(
            cntg[:, :], Lall[:, :], thrb[:, :], 0.0, ALU.is_gt, ALU.add,
            accum_out=cnt[:, :]), inc=("DV", 1))
        dvc = p.v("DV")
        A("pe", lambda e: nc.tensor.matmul(pb[7][0:1, 0:1], cnt[:, :], onescol[:, :],
                                           start=True, stop=True),
          waits=[("DV", dvc)], inc=("PE", 1))
        pev = p.v("PE")
        A("dv", lambda e: nc.vector.tensor_scalar(flag[:, :], pb[7][0:1, 0:1], 511.5,
                                                  None, ALU.is_gt),
          waits=[("PE", pev)], inc=("DV", 1))
        A("dv", lambda e: nc.vector.tensor_sub(dds[:, :], mid[:, :], lo[:, :]),
          inc=("DV", 1))
        A("dv", lambda e: nc.vector.tensor_sub(ees[:, :], hi[:, :], mid[:, :]),
          inc=("DV", 1))
        A("dv", lambda e: nc.vector.scalar_tensor_tensor(
            lo[:, :], flag[:, :], dds[:, :], lo[:, :], ALU.mult, ALU.add),
          inc=("DV", 1))
        A("dv", lambda e: nc.vector.scalar_tensor_tensor(
            hi[:, :], flag[:, :], ees[:, :], mid[:, :], ALU.mult, ALU.add),
          inc=("DV", 1))
    dvm = p.v("DV")
    A("pe", lambda e: nc.tensor.matmul(pb[6][:, 0:1], ones_f[:, :], hi[:, :],
                                       start=True, stop=True),
      waits=[("DV", dvm)], inc=("PE", 1))
    pev = p.v("PE")
    A("dv", lambda e: nc.vector.tensor_copy(thrb[:, :], pb[6][:, 0:1]),
      waits=[("PE", pev)], inc=("DV", 1))
    A("dv", lambda e: nc.vector.tensor_scalar(
        mfull[:, :], Lall[:, :], thrb[:, :], 0.0, ALU.is_gt, ALU.add,
        accum_out=totp[:, :]), inc=("DV", 1))
    A("dv", lambda e: nc.vector.tensor_scalar(
        mown[:, :], logits[:, :], thrb[:, :], 0.0, ALU.is_gt, ALU.add),
      inc=("DV", 1))
    A("dv", lambda e: nc.vector.tensor_scalar(
        umown[:, :], mown[:, :], -1.0, 1.0, ALU.mult, ALU.add), inc=("DV", 1))
    A("dv", lambda e: nc.vector.reduce_sum(tot0[:, :], mfull[:, 0:16], AX.X),
      inc=("DV", 1))
    A("dv", lambda e: nc.vector.tensor_tensor_scan(
        scanf[:, :], mown[:, :], zer16[:, :], 0.0, ALU.add, ALU.add),
      waits=[("DV", DV_ZERO)], inc=("DV", 1))
    dvs = p.v("DV")
    A("pe", lambda e: nc.tensor.matmul(pb[6][:, 0:1], lmask[:, :], totp[:, :],
                                       start=True, stop=True),
      waits=[("DV", dvs)], inc=("PE", 1))
    pev = p.v("PE")
    A("dv", lambda e: nc.vector.tensor_copy(offs[:, :], pb[6][:, 0:1]),
      waits=[("PE", pev)], inc=("DV", 1))
    A("dv", lambda e: nc.vector.tensor_scalar(
        slotF[:, :], scanf[:, :], offs[:, :], -1.0, ALU.add, ALU.add), inc=("DV", 1))
    A("dv", lambda e: nc.vector.tensor_mul(colr[:, :], myrank[:, :], tot0[:, :]),
      inc=("DV", 1))
    A("dv", lambda e: nc.vector.tensor_scalar(
        slotF[:, :], slotF[:, :], colr[:, :], None, ALU.add), inc=("DV", 1))
    A("dv", lambda e: nc.vector.tensor_scalar(
        cntg[:, 0:16], umown[:, :], BIG, None, ALU.mult), inc=("DV", 1))
    A("dv", lambda e: nc.vector.tensor_add(slotF[:, :], slotF[:, :], cntg[:, 0:16]),
      inc=("DV", 1))
    A("dv", lambda e: nc.vector.tensor_copy(slotI[:, :], slotF[:, :]), inc=("DV", 1))
    DV_SLOT = p.v("DV")

    for j in range(16):
        A("gp", (lambda j=j: lambda e: e.dma_start(
            out=xg_dram.rearrange("a b -> (a b)")[j * 65536:(j + 1) * 65536]
            .rearrange("(pp n) -> pp n", pp=128),
            in_=zerot[:, :]))(),
          waits=[("DV", DV_ZERO)] + (list(GD_X1) if j == 0 else []), inc=("LZ", 16))
    GD_XGZ = p.v("LZ")
    gd_sc = {}
    ac_x2 = {}
    for t in range(16):
        sb = t % 2
        xbs = "XB%d" % sb
        A("gp", (lambda t=t, sb=sb: lambda e: e.dma_start(
            out=x1bt[sb][:, :], in_=x1_dram[t * 128:(t + 1) * 128, :]))(),
          waits=list(GD_X1) + [("AC", ac_x2.get(t - 2, 0)),
                               ("PE", PE_WOD if t < 2 else 0)],
          inc=(xbs, 16))
        gdv = (xbs, p.v(xbs))
        A("dv", (lambda sb=sb, t=t: lambda e: nc.vector.tensor_mul(
            tbias[sb][:, :], nm2s[:, t:t + 1], r2s[:, t:t + 1]))(),
          waits=[("DV", DV_SLOT), ("AC", ac_x2.get(t - 2, 0))], inc=("DV", 1))
        dvb = p.v("DV")
        A("ac", (lambda sb=sb, t=t: lambda e: nc.scalar.activation(
            xn2scr[sb][:, :], x1bt[sb][:, :], AF.Identity, bias=tbias[sb][:, :],
            scale=r2s[:, t:t + 1]))(),
          waits=[("DV", dvb), gdv, ("AC", AC_A3 if t < 2 else 0)] + (
              [gd_sc[t - 2]] if (t - 2) in gd_sc else []),
          inc=("AC", 1))
        ac_x2[t] = p.v("AC")
        A("gp", (lambda t=t, sb=sb: lambda e: nc.gpsimd.indirect_dma_start(
            out=xg_dram, out_offset=IndirectOffsetOnAxis(slotI[:, t:t + 1], 0),
            in_=xn2scr[sb][:, :], in_offset=None,
            bounds_check=CAP - 1, oob_is_err=False))(),
          waits=[("AC", ac_x2[t]), ("LZ", GD_XGZ)], inc=("SC%d" % sb, 16))
        gd_sc[t] = ("SC%d" % sb, p.v("SC%d" % sb))
    GD_SCAT = [("SC0", p.v("SC0")), ("SC1", p.v("SC1"))]
    A("gp", lambda e: nc.gpsimd.collective_compute(
        "AllReduce", ALU.add,
        replica_groups=[[0, 1], [2, 3], [4, 5], [6, 7]],
        ins=[xg_dram.opt()], outs=[xg_red.opt()]),
      waits=list(GD_SCAT), inc=("C1", 1))
    for j in range(2):
        A("gp", (lambda j=j: lambda e: nc.gpsimd.indirect_dma_start(
            out=xgtm[j][:, :], out_offset=None,
            in_=xg_red, in_offset=IndirectOffsetOnAxis(slotrow[:, j:j + 1], 0)))(),
          waits=[("C1", 1)], inc=("XG", 16))
    GD_XGL = p.v("XG")
    dv_xgbank = {6: 0, 7: 0}
    for j in range(2):
        for d in range(16):
            bk = 6 + (d % 2)
            A("pe", (lambda j=j, d=d, bk=bk: lambda e: nc.tensor.transpose(
                pb[bk][:, 0:128], xgtm[j][:, d * 128:(d + 1) * 128], identF[:, :]))(),
              waits=[("XG", GD_XGL), ("DV", dv_xgbank[bk])],
              inc=("PE", 1))
            pev = p.v("PE")
            A("dv", (lambda j=j, d=d, bk=bk: lambda e: nc.vector.tensor_copy(
                xgT[d][:, j * 128:(j + 1) * 128], pb[bk][:, 0:128]))(),
              waits=[("PE", pev)], inc=("DV", 1))
            dv_xgbank[bk] = p.v("DV")
    DV_XGT = p.v("DV")

    ac_gelu = {}
    pe_w1 = {}
    dv_w1c = {}
    for fq in range(16):
        for g in range(4):
            k = fq * 4 + g
            sbi = k % 2
            w1s = "W1%d" % sbi
            wv = [("PE", pe_w1.get(k - 2, 0))]
            if k < 2:
                wv += [("DV", DV_XGT), ("PB0", p.v("PB0")), ("PB1", p.v("PB1"))]
            A("sy", (lambda fq=fq, g=g, sbi=sbi: lambda e: e.dma_start(
                out=w1st[sbi][:, :],
                in_=w1bf_dram[:, (fq * 16 + g * 4) * 512:(fq * 16 + g * 4 + 4) * 512]))(),
              waits=wv, inc=(w1s, 16))
            lwv = p.v(w1s)
            for c in range(4):
                d = g * 4 + c
                for j in range(4):
                    wv = [(w1s, lwv), ("DV", DV_XGT)]
                    if d == 0:
                        wv.append(("AC", ac_gelu.get(fq - 1, 0)))
                    A("pe", (lambda sbi=sbi, c=c, j=j, d=d: lambda e: nc.tensor.matmul(
                        pb[j][:, 0:HALFCAP],
                        w1st[sbi][:, (c * 4 + j) * 128:(c * 4 + j + 1) * 128],
                        xgT[d][:, :], start=(d == 0), stop=(d == 15)))(),
                      waits=wv,
                      inc=("PE", 1) if j == 3 else None)
            pe_w1[k] = p.v("PE")
        pew = p.v("PE")
        for j in range(4):
            f = fq * 4 + j
            A("ac", (lambda f=f, j=j: lambda e: nc.scalar.activation(
                h_bf[f][:, :], pb[j][:, 0:HALFCAP], AF.Gelu_apprx_tanh,
                bias=b1rs[:, f:f + 1]))(),
              waits=[("PE", pew)], inc=("AC", 1))
        ac_gelu[fq] = p.v("AC")
    AC_W1 = p.v("AC")
    A("sy", lambda e: e.dma_start(out=b2rows[:, :], in_=b2row),
      waits=[("PE", PE_WOD)], inc=("B2", 16))
    LW_B2 = p.v("B2")
    A("dv", lambda e: nc.vector.tensor_copy(b2row_bf[:, :], b2rows[:, :]),
      waits=[("B2", LW_B2)], inc=("DV", 1))
    PL_B2 = p.v("DV")

    dv_yevac = {}
    gd_y = {}
    pl_w2 = {}
    pe_w2 = {}
    for dm in range(4):
        for g in range(16):
            k = dm * 16 + g
            sbi = k % 2
            w2s = "W2%d" % sbi
            A("sy", (lambda dm=dm, g=g, sbi=sbi: lambda e: e.dma_start(
                out=w2st[sbi][:, :],
                in_=w2bf_dram[:, (dm * 64 + g * 4) * 512:(dm * 64 + g * 4 + 4) * 512]))(),
              waits=[("PE", pe_w2.get(k - 2, 0))], inc=(w2s, 16))
            lwv = p.v(w2s)
            for j in range(4):
                f = g * 4 + j
                for sc in range(2):
                    bk = (dm % 2) * 2 + sc
                    wv = [(w2s, lwv), ("AC", AC_W1)]
                    if f == 0:
                        wv.append(("DV", dv_yevac.get((dm - 2, sc), 0)))
                    A("pe", (lambda f=f, j=j, sc=sc, bk=bk, sbi=sbi:
                             lambda e: nc.tensor.matmul(
                                 pb[bk][:, :], h_bf[f][:, sc * 128:(sc + 1) * 128],
                                 w2st[sbi][:, j * 512:(j + 1) * 512],
                                 start=(f == 0), stop=False))(),
                      waits=wv, inc=("PE", 1) if sc == 1 else None)
            pe_w2[k] = p.v("PE")
        for sc in range(2):
            bk = (dm % 2) * 2 + sc
            A("pe", (lambda dm=dm, bk=bk: lambda e: nc.tensor.matmul(
                pb[bk][:, :], ones_bf[:, :], b2row_bf[0:1, dm * 512:(dm + 1) * 512],
                start=False, stop=True))(),
              waits=[("DV", PL_B2)], inc=("PE", 1))
            pev = p.v("PE")
            A("dv", (lambda sc=sc, bk=bk: lambda e: nc.vector.tensor_copy(
                y_sb[sc][:, :], pb[bk][:, :]))(),
              waits=[("PE", pev)] + (
                  [gd_y[(dm - 1, sc)]] if (dm - 1, sc) in gd_y else []),
              inc=("DV", 1))
            dv_yevac[(dm, sc)] = p.v("DV")
            A("gp", (lambda dm=dm, sc=sc: lambda e: e.dma_start(
                out=ymine[sc * 128:(sc + 1) * 128, dm * 512:(dm + 1) * 512],
                in_=y_sb[sc][:, :]))(),
              waits=[("DV", dv_yevac[(dm, sc)])], inc=("YS%d" % sc, 16))
            gd_y[(dm, sc)] = ("YS%d" % sc, p.v("YS%d" % sc))
    GD_Y = [("YS0", p.v("YS0")), ("YS1", p.v("YS1"))]
    A("gp", lambda e: nc.gpsimd.collective_compute(
        "AllGather", ALU.bypass,
        replica_groups=[[0, 1], [2, 3], [4, 5], [6, 7]],
        ins=[ymine.opt()], outs=[y_full.opt()]),
      waits=list(GD_Y), inc=("C2", 1))

    gd_out = {}
    gd_xb = {}
    gd_yb = {}
    dv_tb = {}
    dv_s2 = {}
    dv_rs = {}
    dv_ybfree = {}
    ac_s1 = {}
    ac_sq = {}
    ac_apply = {}
    pl_add = {}

    def fin_s0(t):
        xb = "XB%d" % (t % 6)
        wv = [("C2", 1)]
        if (t - 6) in pl_add:
            wv.append(("PL", pl_add[t - 6]))
        A("gp", (lambda t=t: lambda e: e.dma_start(
            out=x1bt[t % 6][:, :], in_=x1_dram[t * 128:(t + 1) * 128, :]))(),
          waits=wv, inc=(xb, 16))
        gd_xb[t] = (xb, p.v(xb))
        yb = "YB%d" % (t % 4)
        wv = [("C2", 1)]
        if (t - 4) in dv_ybfree:
            wv.append(("DV", dv_ybfree[t - 4]))
        A("gp", (lambda t=t: lambda e: nc.gpsimd.indirect_dma_start(
            out=ybuf[t % 4][:, :], out_offset=None,
            in_=y_full, in_offset=IndirectOffsetOnAxis(slotI[:, t:t + 1], 0),
            bounds_check=CAP - 1, oob_is_err=False))(),
          waits=wv, inc=(yb, 16))
        gd_yb[t] = (yb, p.v(yb))
        wv = []
        if (t - 4) in ac_s1:
            wv.append(("AC", ac_s1[t - 4]))
        A("dv", (lambda t=t: lambda e: nc.vector.tensor_mul(
            tbias3[t % 4][:, :], nm2s[:, t:t + 1], r2s[:, t:t + 1]))(),
          waits=wv, inc=("DV", 1))
        dv_tb[t] = p.v("DV")

    def fin_s1(t):
        wv = [("DV", dv_tb[t]), gd_xb[t]]
        if t < 2:
            wv.append(("AC", AC_W1))
        if (t - 4) in dv_s2:
            wv.append(("DV", dv_s2[t - 4]))
        A("ac", (lambda t=t: lambda e: nc.scalar.activation(
            xn2scr[t % 4][:, :], x1bt[t % 6][:, :], AF.Identity,
            bias=tbias3[t % 4][:, :], scale=r2s[:, t:t + 1]))(),
          waits=wv, inc=("AC", 1))
        ac_s1[t] = p.v("AC")

    def fin_s2(t):
        A("dv", (lambda t=t: lambda e: nc.vector.tensor_mul(
            gmcol[t % 2][:, :], mown[:, t:t + 1], logits[:, t:t + 1]))(),
          inc=("DV", 1))
        A("dv", (lambda t=t: lambda e: nc.vector.tensor_scalar(
            junk2[:, :], ybuf[t % 4][:, :], gmcol[t % 2][:, :], None,
            ALU.mult))(),
          waits=[gd_yb[t]], inc=("DV", 1))
        dv_ybfree[t] = p.v("DV")
        A("dv", (lambda t=t: lambda e: nc.vector.scalar_tensor_tensor(
            xn2scr[t % 4][:, :], xn2scr[t % 4][:, :], umown[:, t:t + 1],
            junk2[:, :], ALU.mult, ALU.add))(),
          waits=[("AC", ac_s1[t])], inc=("DV", 1))
        A("dv", (lambda t=t: lambda e: nc.vector.reduce_sum(
            tsum3[t % 4][:, :], xn2scr[t % 4][:, :], AX.X, negate=True))(),
          inc=("DV", 1))
        A("dv", (lambda t=t: lambda e: nc.vector.tensor_scalar(
            tsum3[t % 4][:, :], tsum3[t % 4][:, :], 1.0 / D, None,
            ALU.mult))(), inc=("DV", 1))
        dv_s2[t] = p.v("DV")

    def fin_s3a(t):
        A("ac", (lambda t=t: lambda e: nc.scalar.activation(
            junk[:, :], xn2scr[t % 4][:, :], AF.Square,
            bias=tsum3[t % 4][:, :], accum_out=tssq3[t % 4][:, :]))(),
          waits=[("DV", dv_s2[t]), ("DV", dv_rs.get(t - 4, 0))],
          inc=("AC", 1))
        ac_sq[t] = p.v("AC")

    def fin_s3b(t):
        A("dv", (lambda t=t: lambda e: nc.vector.tensor_scalar(
            trstd3[t % 4][:, :], tssq3[t % 4][:, :], 1.0 / D, EPS,
            ALU.mult, ALU.add))(),
          waits=[("AC", ac_sq[t]), ("AC", ac_apply.get(t - 4, 0))],
          inc=("DV", 1))
        dv_rs[t] = p.v("DV")

    def fin_s4(t):
        A("ac", (lambda t=t: lambda e: nc.scalar.activation(
            trstd3[t % 4][:, :], trstd3[t % 4][:, :], AF.Ln))(),
          waits=[("DV", dv_rs[t])], inc=("AC", 1))
        A("ac", (lambda t=t: lambda e: nc.scalar.activation(
            trstd3[t % 4][:, :], trstd3[t % 4][:, :], AF.Exp, scale=-0.5))(),
          inc=("AC", 1))
        A("ac", (lambda t=t: lambda e: nc.scalar.activation(
            tbias3b[t % 2][:, :], tsum3[t % 4][:, :], AF.Identity,
            scale=trstd3[t % 4][:, :]))(), inc=("AC", 1))
        wv = []
        if (t - 2) in pl_add:
            wv.append(("PL", pl_add[t - 2]))
        A("ac", (lambda t=t: lambda e: nc.scalar.activation(
            yout[t % 2][:, :], xn2scr[t % 4][:, :], AF.Identity,
            bias=tbias3b[t % 2][:, :], scale=trstd3[t % 4][:, :]))(),
          waits=wv, inc=("AC", 1))
        ac_apply[t] = p.v("AC")

    def fin_s5(t):
        wv = [("AC", ac_apply[t])]
        if (t - 2) in gd_out:
            wv.append(gd_out[t - 2])
        A("pl", (lambda t=t: lambda e: nc.gpsimd.tensor_add(
            outst[t % 2][:, :], yout[t % 2][:, :], x1bt[t % 6][:, :]))(),
          waits=wv, inc=("PL", 1))
        pl_add[t] = p.v("PL")
        A("gp", (lambda t=t: lambda e: e.dma_start(
            out=out[t * 128:(t + 1) * 128, :], in_=outst[t % 2][:, :]))(),
          waits=[("PL", pl_add[t])], inc=("OU%d" % (t % 2), 16))
        gd_out[t] = ("OU%d" % (t % 2), p.v("OU%d" % (t % 2)))

    for s in range(21):
        if s < 16:
            fin_s0(s)
        if 1 <= s <= 16:
            fin_s1(s - 1)
        if 2 <= s <= 17:
            fin_s2(s - 2)
        if 3 <= s <= 18:
            fin_s3a(s - 3)
        if 4 <= s <= 19:
            fin_s4(s - 4)
        if 3 <= s <= 18:
            fin_s3b(s - 3)
        if 5 <= s <= 20:
            fin_s5(s - 5)
    A("gp", lambda e: nc.gpsimd.memset(zerot[0:1, 0:1], 0.0),
      waits=[("OU0", p.v("OU0")), ("OU1", p.v("OU1"))], inc=None)

    semstack = contextlib.ExitStack()
    with nc.Block() as block:
        sems = {n: semstack.enter_context(nc.semaphore("sem_" + n))
                for n in p.cnt}

        @block.sync
        def _(eng):
            p.play("sy", eng, sems)

        @block.tensor
        def _(eng):
            p.play("pe", eng, sems)

        @block.vector
        def _(eng):
            p.play("dv", eng, sems)

        @block.scalar
        def _(eng):
            p.play("ac", eng, sems)

        @block.gpsimd
        def _(eng):
            p.play(("gp", "pl"), eng, sems)

    semstack.close()
    stack.close()
    return nc


def _host_inputs(inputs):
    x = np.asarray(inputs["L0_attn_x"], dtype=np.float32)
    Wq = np.asarray(inputs["L0_attn_Wq"], dtype=np.float32)
    Wkv = np.asarray(inputs["L0_attn_Wkv"], dtype=np.float32)
    Wo = np.asarray(inputs["L0_attn_Wo"], dtype=np.float32)
    qk = np.asarray(inputs["L0_attn_qk_temp"], dtype=np.float32).reshape(16, 1)
    W_r = np.asarray(inputs["L0_ffn_W_r"], dtype=np.float32)
    b_r = np.asarray(inputs["L0_ffn_b_r"], dtype=np.float32).reshape(1, 1)
    W1 = np.asarray(inputs["L0_ffn_W1"], dtype=np.float32)
    b1 = np.asarray(inputs["L0_ffn_b1"], dtype=np.float32)
    W2 = np.asarray(inputs["L0_ffn_W2"], dtype=np.float32)
    b2 = np.asarray(inputs["L0_ffn_b2"], dtype=np.float32)

    bf = ml_dtypes.bfloat16
    wqT = np.ascontiguousarray(Wq.T)
    wkvT = np.ascontiguousarray(Wkv.T)
    woT = np.ascontiguousarray(Wo.T)
    w1h = np.ascontiguousarray(
        W1.reshape(16, 128, 16, 4, 128).transpose(1, 2, 0, 3, 4).reshape(128, -1))
    w2h = np.ascontiguousarray(
        W2.reshape(64, 128, 4, 512).transpose(1, 2, 0, 3).reshape(128, -1))
    b1r = np.ascontiguousarray(b1.reshape(64, 128).T)
    b2row = np.ascontiguousarray(b2.reshape(1, D))
    wr_row = np.ascontiguousarray(W_r.reshape(1, D))

    identB = np.eye(128, dtype=bf)
    identF = np.eye(128, dtype=np.float32)
    r = np.arange(128)
    maskA = np.where(r[None, :] > r[:, None], 0.0, NEG)
    maskB = np.where(r[:, None] >= r[None, :], 0.0, NEG)
    maskAB = np.concatenate(
        [maskA, np.zeros((128, 128)), maskB], axis=1).astype(bf)
    ones_bf = np.ones((1, 128), dtype=bf)
    ones_f = np.ones((1, 128), dtype=np.float32)
    onescol = np.ones((128, 1), dtype=np.float32)
    bigoh = np.zeros((128, 127), dtype=bf)
    bigoh[:, 63] = 1.0
    lmask = (r[:, None] < r[None, :]).astype(np.float32)
    selq = np.ascontiguousarray(np.kron(np.eye(16), np.ones((1, 128)))).astype(bf)

    inv_freq = 1.0 / (10000.0 ** (np.arange(0, DH, 2, dtype=np.float64) / DH))

    maps = []
    for c in range(8):
        b, hh = c // 2, c % 2
        t0 = hh * T
        xp = np.zeros((TK, D), dtype=np.float32)
        lo_ = max(0, t0 - HALO)
        xp[HALO - (t0 - lo_):HALO] = x[b, lo_:t0]
        xp[HALO:] = x[b, t0:t0 + T]
        pos = (t0 - HALO) + np.arange(TK, dtype=np.float64)
        ang = pos[None, :] * inv_freq[:, None]
        cosb = np.ascontiguousarray(
            np.concatenate([np.cos(ang), np.cos(ang)], axis=0).astype(bf))
        sinb = np.ascontiguousarray(
            np.concatenate([np.sin(ang), -np.sin(ang)], axis=0).astype(bf))
        kb = np.zeros((1, TK), dtype=bf)
        if hh == 0:
            kb[0, :HALO] = NEG
        myrank = np.full((128, 1), float(hh), dtype=np.float32)
        slotrow = np.ascontiguousarray(np.stack(
            [hh * 256 + np.arange(128), hh * 256 + 128 + np.arange(128)],
            axis=1).astype(np.int32))
        maps.append({
            "x_in": xp, "wqT": wqT, "wkvT": wkvT, "woT": woT, "w1h": w1h, "w2": w2h,
            "b1r": b1r, "b2row": b2row, "wr_row": wr_row, "br": b_r, "qk16": qk,
            "cosb": cosb, "sinb": sinb, "kbias": kb, "identB": identB,
            "identF": identF, "maskAB": maskAB, "ones_bf": ones_bf,
            "ones_f": ones_f, "onescol": onescol, "bigoh": bigoh, "lmask": lmask,
            "myrank": myrank, "slotrow": slotrow, "selq": selq,
        })
    return maps


def kernel(**inputs):
    maps = _host_inputs(inputs)
    nc = build_nc()
    res = run_bass_kernel_spmd(nc, maps, core_ids=list(range(8)))
    outp = np.empty((4, 4096, 2048), dtype=np.float32)
    for c in range(8):
        b, hh = c // 2, c % 2
        outp[b, hh * T:(hh + 1) * T] = np.asarray(res.results[c]["out"])
    return outp
